# version 01
# baseline (speedup 1.0000x reference)
import contextlib
import numpy as np

import concourse.bass as bass
import concourse.mybir as mybir
from concourse.bass_utils import run_bass_kernel_spmd
from concourse.bass_core_rust import IndirectOffsetOnAxis

F32 = mybir.dt.float32
BF16 = mybir.dt.bfloat16
I32 = mybir.dt.int32
U8 = mybir.dt.uint8
AF = mybir.ActivationFunctionType
ALU = mybir.AluOpType
AX = mybir.AxisListType

BATCH, T, D = 16, 2048, 1024
NCORES, BPC = 8, 2
TOK = BPC * T
NT = TOK // 128
NTB = T // 128

E0, I0 = 8, 512
CAP0 = 768
SLOTS0 = E0 * CAP0
E1, DFF1, CAP1 = 16, 2048, 256
DIN, DSTATE, DTRANK, DCONV = 2048, 16, 64, 4
LN_EPS = 1e-5



class Tok:
    __slots__ = ("eng", "dma", "idx", "need_inc", "inc_val", "ndma")

    def __init__(self, eng, dma, idx, ndma):
        self.eng, self.dma, self.idx, self.ndma = eng, dma, idx, ndma
        self.need_inc = False
        self.inc_val = None


class Sched:

    ENGINES = ("V", "S", "T", "G", "Q")

    def __init__(self, P):
        self.P = P
        self.items = {e: [] for e in self.ENGINES}

    def op(self, eng, fn, deps=(), dma=False, ndma=1, flush=False):
        if flush:
            inner = fn
            fn = lambda stream: (inner(stream), self.P.flushers[eng](stream))[-1]
        t = Tok(eng, dma, len(self.items[eng]), ndma if dma else 0)
        self.items[eng].append((fn, [d for d in deps if d is not None], t))
        return t

    def emit(self):
        P = self.P
        for eng in self.ENGINES:
            for fn, deps, tok in self.items[eng]:
                for d in deps:
                    if d.eng != eng or d.dma or eng == "G":
                        d.need_inc = True
        for eng in self.ENGINES:
            cnt_c, cnt_d = 0, 0
            for fn, deps, tok in self.items[eng]:
                if tok.dma:
                    tok.need_inc = True
                    cnt_d += 16 * tok.ndma
                    tok.inc_val = cnt_d
                else:
                    if tok.need_inc:
                        cnt_c += 1
                        tok.inc_val = cnt_c
        sems = {}
        for eng in self.ENGINES:
            sems[eng + "c"] = P.sem("sem" + eng + "c")
            sems[eng + "d"] = P.sem("sem" + eng + "d")

        def emit_stream(eng, stream):
            waited = {}
            for fn, deps, tok in self.items[eng]:
                for d in deps:
                    if d.eng == eng and not d.dma and eng != "G":
                        continue
                    key = d.eng + ("d" if d.dma else "c")
                    v = d.inc_val
                    assert v is not None, "dependency on un-inc'd token"
                    if waited.get(key, 0) < v:
                        stream.wait_ge(sems[key].h, v)
                        waited[key] = v
                ret = fn(stream)
                if tok.need_inc:
                    key = tok.eng + ("d" if tok.dma else "c")
                    if tok.dma:
                        insts = ret if isinstance(ret, (list, tuple)) else [ret]
                        assert len(insts) == tok.ndma, (len(insts), tok.ndma)
                        for ins in insts:
                            ins.then_inc(sems[key].h, 16)
                    else:
                        ins = ret[-1] if isinstance(ret, (list, tuple)) else ret
                        ins.then_inc(sems[key].h, 1)

        blk = P.block
        for eng, attr in (("V", "vector"), ("S", "scalar"), ("T", "tensor"),
                          ("G", "gpsimd"), ("Q", "sync")):
            if not self.items[eng]:
                continue
            getattr(blk, attr)(lambda stream, e=eng: emit_stream(e, stream))


class Prog:
    def __init__(self):
        self.nc = bass.Bass("TRN2", target_bir_lowering=False,
                            detect_race_conditions=False)
        self.es = contextlib.ExitStack()
        self.es.__enter__()
        self.sems = {}
        self.dram = {}
        self.uid = 0

    def sem(self, name):
        if name not in self.sems:
            self.sems[name] = _SemH(self.es, self.nc, name)
        return self.sems[name]

    def dram_in(self, name, shape, dtype=F32):
        self.dram[name] = self.nc.dram_tensor(name, list(shape), dtype,
                                              kind="ExternalInput").ap()
        return self.dram[name]

    def dram_out(self, name, shape, dtype=F32):
        self.dram[name] = self.nc.dram_tensor(name, list(shape), dtype,
                                              kind="ExternalOutput").ap()
        return self.dram[name]

    def dram_tmp(self, name, shape, dtype=F32):
        self.dram[name] = self.nc.dram_tensor(name, list(shape), dtype,
                                              kind="Internal").ap()
        return self.dram[name]

    def sb(self, name, shape, dtype):
        self.uid += 1
        return self.es.enter_context(
            self.nc.sbuf_tensor(f"{name}_{self.uid}", list(shape), dtype))

    def ps(self, name, shape, dtype=F32):
        self.uid += 1
        return self.es.enter_context(
            self.nc.psum_tensor(f"{name}_{self.uid}", list(shape), dtype))

    def open_block(self):
        self._block_cm = self.nc.Block()
        self.block = self._block_cm.__enter__()

    def make_psum_pool(self):
        self.PB0 = self.ps("psum_b0", [128, 1024], F32)
        self.PB1 = self.ps("psum_b1", [128, 1024], F32)
        self.PB2 = self.ps("psum_b2", [128, 1024], F32)
        self.PB3 = self.ps("psum_b3", [128, 1024], F32)
        self.PM0 = self.PB2[:, 0:512]
        self.PM1 = self.PB3[:, 0:512]

    def close(self):
        self._block_cm.__exit__(None, None, None)
        self.es.__exit__(None, None, None)


class _SemH:
    def __init__(self, es, nc, name):
        self.h = es.enter_context(nc.semaphore(name))


def sbs(P, scope, name, shape, dtype):
    P.uid += 1
    return scope.enter_context(
        P.nc.sbuf_tensor(f"{name}_{P.uid}", list(shape), dtype))


def greg(P, g, val):
    r = P.__dict__.setdefault("_gregs", {})
    if val not in r:
        r[val] = g.to_reg(val)
    return r[val]



def make_consts(P, sch, lite=False):
    c = {}
    c["ident_bf"] = P.sb("c_identbf", [128, 128], BF16)
    if not lite:
        c["ident_f32"] = P.sb("c_identf", [128, 128], F32)
        c["ustrict_f32"] = P.sb("c_ustrict", [128, 128], F32)
        c["iota8"] = P.sb("c_iota8", [128, 8], F32)
        c["iota16"] = P.sb("c_iota16", [128, 16], F32)
        c["iota256"] = P.sb("c_iota256", [128, 256], F32)
        c["iotap"] = P.sb("c_iotap", [128, 1], F32)
        c["zrow"] = P.sb("c_zrow", [1, D], F32)
    c["ones1_bf"] = P.sb("c_ones1bf", [1, 128], BF16)
    c["ones1_f32"] = P.sb("c_ones1f", [1, 128], F32)
    c["onesc_f32"] = P.sb("c_onescf", [128, 1], F32)
    c["big8"] = P.sb("c_big8", [128, 8], F32)
    c["eps_col"] = P.sb("c_eps", [128, 1], F32)
    c["fl_src"] = P.sb("c_flsrc", [128, 512], F32)
    c["flv_b"] = P.sb("c_flvb", [128, 512], F32)
    c["fls_b"] = P.sb("c_flsb", [128, 512], F32)
    c["flg_b"] = P.sb("c_flgb", [128, 512], F32)
    P.flushers = {
        "V": lambda v: v.tensor_copy(out=c["flv_b"][:], in_=c["fl_src"][:]),
        "S": lambda s: s.activation(out=c["fls_b"][:], in_=c["fl_src"][:],
                                    func=AF.Copy),
        "G": lambda g: g.tensor_copy(out=c["flg_b"][:], in_=c["fl_src"][:]),
    }
    c["one_col"] = P.sb("c_one", [128, 1], F32)
    c["half_col"] = P.sb("c_half", [128, 1], F32)

    def f_memsets(gp):
        gp.memset(c["ident_bf"][:], 0.0)
        if not lite:
            gp.memset(c["ident_f32"][:], 0.0)
            gp.memset(c["ustrict_f32"][:], 0.0)
            gp.memset(c["zrow"][:], 0.0)
        gp.memset(c["ones1_bf"][:], 1.0)
        gp.memset(c["ones1_f32"][:], 1.0)
        gp.memset(c["onesc_f32"][:], 1.0)
        gp.memset(c["eps_col"][:], LN_EPS)
        gp.memset(c["fl_src"][:], 0.0)
        gp.memset(c["flv_b"][:], 0.0)
        gp.memset(c["fls_b"][:], 0.0)
        gp.memset(c["flg_b"][:], 0.0)
        gp.memset(c["one_col"][:], 1.0)
        gp.memset(c["half_col"][:], 0.5)
        return gp.memset(c["big8"][:], 999.0)

    def f_iotas(gp):
        if not lite:
            for nm, n in (("iota8", 8), ("iota16", 16), ("iota256", 256)):
                gp.iota(c[nm][:], pattern=[[1, n]], base=0,
                        channel_multiplier=0,
                        allow_small_or_imprecise_dtypes=True)
            gp.iota(c["iotap"][:], pattern=[[1, 1]], base=0,
                    channel_multiplier=1,
                    allow_small_or_imprecise_dtypes=True)
        return gp.memset(c["big8"][:], 999.0)

    def f_diag(gp):
        gp.affine_select(out=c["ident_bf"][:], in_=c["ident_bf"][:],
                         compare_op=ALU.not_equal, fill=1.0, base=0,
                         pattern=[[-1, 128]], channel_multiplier=1)
        if lite:
            return gp.memset(c["big8"][:], 999.0)
        gp.affine_select(out=c["ident_f32"][:], in_=c["ident_f32"][:],
                         compare_op=ALU.not_equal, fill=1.0, base=0,
                         pattern=[[-1, 128]], channel_multiplier=1)
        return gp.affine_select(out=c["ustrict_f32"][:], in_=c["ustrict_f32"][:],
                                compare_op=ALU.is_ge, fill=1.0, base=0,
                                pattern=[[-1, 128]], channel_multiplier=1)

    t1 = sch.op("G", f_memsets, flush=True)
    t2 = sch.op("G", f_iotas, flush=True)
    t3 = sch.op("G", f_diag, deps=[t1, t2], flush=True)
    P.consts = c
    P.consts_tok = t3
    P.consts_toks = [t1, t2, t3]



def emit_ln(sch, w, xt_buf, xt_tok, xn_out, xn_guard=None, out_bf=None):
    xsq, s1, s2, mu, ex2, var = (w["xsq"], w["s1"], w["s2"], w["mu"],
                                 w["ex2"], w["var"])
    sd, rstd = w["sd"], w["rstd"]
    dinv = 1.0 / xt_buf.shape[1]

    t_sq = sch.op("S", lambda s: s.activation(
        out=xsq[:], in_=xt_buf[:], func=AF.Square),
        deps=[xt_tok, w.get("prev_ln")])

    fl = w["flush_v"]

    def f_stats(v):
        v.reduce_sum(out=s1[:], in_=xt_buf[:], axis=AX.X)
        v.reduce_sum(out=s2[:], in_=xsq[:], axis=AX.X)
        fl(v)
        v.tensor_scalar(out=mu[:], in0=s1[:], scalar1=dinv, scalar2=None,
                        op0=ALU.mult)
        v.tensor_scalar(out=ex2[:], in0=s2[:], scalar1=dinv, scalar2=None,
                        op0=ALU.mult)
        fl(v)
        v.tensor_tensor(out=var[:], in0=mu[:], in1=mu[:], op=ALU.mult)
        fl(v)
        return v.tensor_tensor(out=var[:], in0=ex2[:], in1=var[:],
                               op=ALU.subtract)
    t_stats = sch.op("V", f_stats, deps=[t_sq], flush=True)

    t_sd = sch.op("S", lambda s: s.activation(
        out=sd[:], in_=var[:], func=AF.Sqrt,
        bias=w["eps_col"][:, 0:1]), deps=[t_stats, w.get("ct")], flush=True)

    def f_norm(v):
        v.reciprocal(out=rstd[:], in_=sd[:])
        w["flush_v"](v)
        r = None
        if xn_out is not None:
            r = v.tensor_scalar(out=xn_out[:], in0=xt_buf[:],
                                scalar1=mu[:, 0:1], scalar2=rstd[:, 0:1],
                                op0=ALU.subtract, op1=ALU.mult)
        if out_bf is not None:
            r = v.tensor_scalar(out=out_bf[:], in0=xt_buf[:],
                                scalar1=mu[:, 0:1], scalar2=rstd[:, 0:1],
                                op0=ALU.subtract, op1=ALU.mult)
        return r
    t_xn = sch.op("V", f_norm, deps=[t_sd, xn_guard])
    w["prev_ln"] = t_xn
    return t_xn


def ln_scratch(P, tag, scope=None):
    mk = ((lambda n, sh, dt: sbs(P, scope, n, sh, dt)) if scope is not None
          else (lambda n, sh, dt: P.sb(n, sh, dt)))
    return dict(xsq=mk(f"{tag}_xsq", [128, D], F32),
                s1=mk(f"{tag}_s1", [128, 1], F32),
                s2=mk(f"{tag}_s2", [128, 1], F32),
                mu=mk(f"{tag}_mu", [128, 1], F32),
                ex2=mk(f"{tag}_ex2", [128, 1], F32),
                var=mk(f"{tag}_var", [128, 1], F32),
                sd=mk(f"{tag}_sd", [128, 1], F32),
                rstd=mk(f"{tag}_rstd", [128, 1], F32),
                eps_col=P.consts["eps_col"], ct=P.consts_tok,
                flush_v=P.flushers["V"])



def emit_l0_ffn(P, sch, src, dst, tag, zero_slots=True, stages="ABC"):
    c = P.consts
    src_t = src.rearrange("(n p) d -> n p d", p=128)
    dst_t = dst.rearrange("(n p) d -> n p d", p=128)
    slots = P.dram["l0_slots"]
    oslots = P.dram["l0_oslots"]
    gwT = P.dram["l0_gwT"]
    wgT, wuT, wdT = P.dram["l0_wgT"], P.dram["l0_wuT"], P.dram["l0_wdT"]

    NB = 3
    x_t = [P.sb(f"{tag}_x{j}", [128, D], F32) for j in range(NB)]
    xn_f = P.sb(f"{tag}_xn", [128, D], F32)
    xs_bf = [P.sb(f"{tag}_xs{j}", [128, D], BF16) for j in range(NB)]
    sidx = [P.sb(f"{tag}_sidx{j}", [128, 1], I32) for j in range(NB)]
    w = ln_scratch(P, tag)
    fm = P.sb(f"{tag}_fm", [128, D], F32)
    lg = P.sb(f"{tag}_lg", [128, E0], F32)
    rmax = P.sb(f"{tag}_rmax", [128, 1], F32)
    rw = P.sb(f"{tag}_rw", [128, 1], F32)
    msk = P.sb(f"{tag}_msk", [128, E0], U8)
    cand = P.sb(f"{tag}_cand", [128, E0], F32)
    esel = P.sb(f"{tag}_esel", [128, 1], F32)
    oh = P.sb(f"{tag}_oh", [128, E0], F32)
    rnk8 = P.sb(f"{tag}_rnk8", [128, E0], F32)
    rsel = P.sb(f"{tag}_rsel", [128, 1], F32)
    slotf = P.sb(f"{tag}_slot", [128, 1], F32)
    ovf = P.sb(f"{tag}_ovf", [128, 1], F32)
    sidx_f = P.sb(f"{tag}_sidxf", [128, 1], F32)
    cnt8 = P.sb(f"{tag}_cnt8", [1, E0], F32)
    gw_sb = P.sb(f"{tag}_gw", [128, 8 * E0], F32)
    gidx_all = P.sb(f"{tag}_gidxall", [128, NT], F32)
    esel_all = P.sb(f"{tag}_eselall", [128, NT], F32)
    rsel_all = P.sb(f"{tag}_rselall", [128, NT], F32)
    lgmx_all = P.sb(f"{tag}_lgmxall", [128, NT], F32)
    P._dbg_esel, P._dbg_rsel, P._dbg_lgmx = esel_all, rsel_all, lgmx_all
    lg_all = P.sb(f"{tag}_lgall", [128, NT * E0], F32)
    P._dbg_lg, P._dbg_fm, P._dbg_gw = lg_all, fm, gw_sb

    P._dbg_gidx = gidx_all
    p_tp = P.PB0[:, 0:D]
    p_lg = P.PM0[:, 0:E0]
    p_tot = P.PM0[0:1, 16:16 + E0]
    p_rnk = P.PM1[:, 0:E0]

    CT = P.consts_tok

    t_gw = sch.op("Q", lambda q: q.dma_start(
        out=gw_sb[:].rearrange("p (k e) -> p k e", k=8),
        in_=gwT.rearrange("(k p) e -> p k e", p=128)), dma=True)
    t_zrow = sch.op("Q", lambda q: q.dma_start(
        out=oslots[SLOTS0:SLOTS0 + 1, :], in_=c["zrow"][:]),
        deps=[CT], dma=True)
    t_cnt0 = sch.op("V", lambda v: v.memset(cnt8[:], 0.0), deps=[CT],
                    flush=True)
    t_zs = None
    if zero_slots:
        zbf = P.sb(f"{tag}_zbf", [128, D], BF16)
        t_zb = sch.op("V", lambda v: v.memset(zbf[:], 0.0), deps=[CT],
                      flush=True)
        zs_toks = []
        slots_t = slots.rearrange("(n p) d -> n p d", p=128)
        for zi in range(SLOTS0 // 128):
            zs_toks.append(sch.op("Q", lambda q, zi=zi: q.dma_start(
                out=slots_t[zi], in_=zbf[:]), deps=[t_zb], dma=True))
        t_zs = zs_toks[-1]
        t_zs_all = zs_toks
    else:
        t_zs_all = []

    t_scat = [None] * NT
    t_xload = [None] * NT
    t_xn = [None] * NT
    t_gidx = [None] * NT
    prev = dict(tp_evac=None, lg_cons=None, rank_cons=None, cnt=t_cnt0,
                rw=None, xn_cons_T=None, fm_cons=None, rkm=None, i32=None)

    for i in range(NT):
        ib = i % NB
        t_xload[i] = sch.op("Q", lambda q, i=i, ib=ib: q.dma_start(
            out=x_t[ib][:], in_=src_t[i]),
            deps=[t_xn[i - NB] if i >= NB else None], dma=True)
        t_xn[i] = emit_ln(sch, w, x_t[ib], t_xload[i], xn_f,
                          xn_guard=prev["xn_cons_T"])
        t_tp = sch.op("T", lambda t, i=i: [
            t.transpose(p_tp[:, k * 128:(k + 1) * 128],
                        xn_f[:, k * 128:(k + 1) * 128], c["ident_f32"][:])
            for k in range(8)][-1], deps=[t_xn[i], prev["tp_evac"], CT])
        prev["xn_cons_T"] = t_tp
        t_fm = sch.op("S", lambda s: s.activation(
            out=fm[:], in_=p_tp[:], func=AF.Copy),
            deps=[t_tp, prev["fm_cons"]])
        prev["tp_evac"] = t_fm
        t_lgm = sch.op("T", lambda t: [
            t.matmul(p_lg[:], fm[:, k * 128:(k + 1) * 128],
                     gw_sb[:, k * E0:(k + 1) * E0],
                     start=(k == 0), stop=(k == 7)) for k in range(8)][-1],
            deps=[t_fm, prev["lg_cons"], t_gw])
        prev["fm_cons"] = t_lgm

        def f_sel(v):
            FV = P.flushers["V"]
            v.tensor_copy(out=lg[:], in_=p_lg[:])
            FV(v)
            v.reduce_max(out=rmax[:], in_=lg[:], axis=AX.X)
            FV(v)
            v.tensor_scalar(out=msk[:], in0=lg[:], scalar1=rmax[:, 0:1],
                            scalar2=None, op0=ALU.is_equal)
            FV(v)
            v.select(out=cand[:], mask=msk[:], on_true=c["iota8"][:],
                     on_false=c["big8"][:])
            FV(v)
            v.tensor_reduce(out=esel[:], in_=cand[:], axis=AX.X, op=ALU.min)
            FV(v)
            return v.tensor_scalar(out=oh[:], in0=c["iota8"][:],
                                   scalar1=esel[:, 0:1], scalar2=None,
                                   op0=ALU.is_equal)
        t_sel = sch.op("V", f_sel, deps=[t_lgm, prev["rkm"], prev["rw"], CT],
                       flush=True)
        prev["lg_cons"] = t_sel
        t_rw = sch.op("S", lambda s: s.activation(
            out=rw[:], in_=rmax[:], func=AF.Sigmoid), deps=[t_sel, prev["rw"]],
            flush=True)
        t_rkm = sch.op("T", lambda t: (
            t.matmul(p_rnk[:], c["ustrict_f32"][:], oh[:], start=True, stop=False),
            t.matmul(p_rnk[:], c["ones1_f32"][:], cnt8[:], start=False, stop=True),
            t.matmul(p_tot[:], c["onesc_f32"][:], oh[:], start=True, stop=True)
        )[-1], deps=[t_sel, prev["cnt"], prev["rank_cons"], CT])
        prev["rkm"] = t_rkm

        def f_slot(v, i=i):
            FV = P.flushers["V"]
            v.tensor_tensor(out=cnt8[:], in0=cnt8[:], in1=p_tot[:], op=ALU.add)
            v.tensor_tensor(out=rnk8[:], in0=oh[:], in1=p_rnk[:], op=ALU.mult)
            FV(v)
            v.reduce_sum(out=rsel[:], in_=rnk8[:], axis=AX.X)
            FV(v)
            v.scalar_tensor_tensor(out=slotf[:], in0=esel[:], scalar=float(CAP0),
                                   in1=rsel[:], op0=ALU.mult, op1=ALU.add)
            v.tensor_scalar(out=ovf[:], in0=rsel[:], scalar1=float(CAP0),
                            scalar2=None, op0=ALU.is_ge)
            FV(v)
            v.scalar_tensor_tensor(out=sidx_f[:], in0=ovf[:], scalar=1.0e6,
                                   in1=slotf[:], op0=ALU.mult, op1=ALU.add)
            FV(v)
            v.tensor_copy(out=lg_all[:, i * E0:(i + 1) * E0], in_=lg[:])
            v.tensor_copy(out=esel_all[:, i:i + 1], in_=esel[:])
            v.tensor_copy(out=rsel_all[:, i:i + 1], in_=rsel[:])
            v.tensor_copy(out=lgmx_all[:, i:i + 1], in_=rmax[:])
            return v.tensor_scalar(out=gidx_all[:, i:i + 1], in0=sidx_f[:],
                                   scalar1=float(SLOTS0), scalar2=None,
                                   op0=ALU.min)
        t_slot = sch.op("V", f_slot, deps=[t_rkm, prev["i32"]], flush=True)
        prev["cnt"] = t_slot
        prev["rank_cons"] = t_slot
        t_gidx[i] = t_slot
        t_xs = sch.op("V", lambda v, ib=ib: v.tensor_scalar(
            out=xs_bf[ib][:], in0=xn_f[:], scalar1=rw[:, 0:1], scalar2=None,
            op0=ALU.mult),
            deps=[t_rw, t_scat[i - NB] if i >= NB else None])
        t_i32 = sch.op("S", lambda s, ib=ib: s.activation(
            out=sidx[ib][:], in_=sidx_f[:], func=AF.Copy),
            deps=[t_slot, t_scat[i - NB] if i >= NB else None], flush=True)
        prev["i32"] = t_i32
        prev["rw"] = t_xs
        t_scat[i] = sch.op("G", lambda g, ib=ib: g.indirect_dma_start(
            out=slots, out_offset=IndirectOffsetOnAxis(sidx[ib][:, 0:1], 0),
            in_=xs_bf[ib][:], in_offset=None,
            bounds_check=greg(P, g, SLOTS0 - 1), oob_is_err=False),
            deps=[t_xs, t_i32] + t_zs_all, dma=True)

    def copy_out():
        toks = []
        cbuf = [P.sb(f"{tag}_cp{j}", [128, D], F32) for j in range(2)]
        pv = [None, None]
        for i in range(NT):
            jb = i % 2
            tl = sch.op("Q", lambda q, i=i, jb=jb: q.dma_start(
                out=cbuf[jb][:], in_=src_t[i]), deps=[pv[jb]] + t_scat, dma=True)
            ts_ = sch.op("Q", lambda q, i=i, jb=jb: q.dma_start(
                out=dst_t[i], in_=cbuf[jb][:]), deps=[tl], dma=True)
            pv[jb] = ts_
            toks.append(ts_)
        return toks

    if stages == "A":
        return copy_out()

    NCH = CAP0 // 256
    NCC = E0 * NCH
    wg_bf = [P.sb(f"{tag}_wg{j}", [128, 8 * I0], BF16) for j in range(2)]
    wu_bf = [P.sb(f"{tag}_wu{j}", [128, 8 * I0], BF16) for j in range(2)]
    wd_bf = [P.sb(f"{tag}_wd{j}", [128, 4 * D], BF16) for j in range(2)]
    st_tm = [P.sb(f"{tag}_sttm{j}", [128, D], BF16) for j in range(4)]
    xe_fm = [P.sb(f"{tag}_xefm{j}", [128, 8 * 256], BF16) for j in range(2)]
    gbuf = P.sb(f"{tag}_gbuf", [128, 256], BF16)
    ubuf = P.sb(f"{tag}_ubuf", [128, 256], BF16)
    pbuf = [P.sb(f"{tag}_pbuf{j}", [128, 256], BF16) for j in range(4)]
    o_sb = [P.sb(f"{tag}_osb{j}", [128, D], F32) for j in range(2)]

    p_g = [P.PM0[:, 0:256], P.PM0[:, 256:512]]
    p_u = P.PM1[:, 0:256]
    p_o = [P.PB0[:, 0:D], P.PB1[:, 0:D]]
    p_tr = [P.PB0[:, 0:D], P.PB1[:, 0:D]]

    def wslice(arr, e, nk):
        return arr[e].rearrange("(k p) m -> p k m", p=128)

    def wdst(buf, nk):
        return buf[:].rearrange("p (k m) -> p k m", k=nk)

    def w_load(e):
        d0 = [last_gu_of_expert.get(e - 2), last_down_of_expert.get(e - 2),
              wload_chain[0]]
        ta = sch.op("G", lambda g, e=e: g.dma_start(
            out=wdst(wg_bf[e % 2], 8), in_=wslice(wgT, e, 8)), deps=d0, dma=True)
        tb = sch.op("G", lambda g, e=e: g.dma_start(
            out=wdst(wu_bf[e % 2], 8), in_=wslice(wuT, e, 8)), deps=[ta], dma=True)
        tc = sch.op("G", lambda g, e=e: g.dma_start(
            out=wdst(wd_bf[e % 2], 4), in_=wslice(wdT, e, 4)), deps=[tb], dma=True)
        wload_chain[0] = tc
        return tc

    def sl_load(cc, dep):
        e, ch = cc // NCH, cc % NCH
        toks = []
        for j in range(2):
            r0 = e * CAP0 + ch * 256 + j * 128
            toks.append(sch.op("Q", lambda q, r0=r0, b=(2 * cc + j) % 4:
                               q.dma_start(out=st_tm[b][:],
                                           in_=slots[r0:r0 + 128, :]),
                               deps=dep, dma=True))
        return toks

    last_gu_of_expert = {}
    last_down_of_expert = {}
    wload_chain = [None]
    if (stages.startswith("ABt") and "W" not in stages) or stages == "ABl":
        t_wl = {}
    elif "W" in stages and stages.endswith("w1"):
        t_wl = {0: sch.op("G", lambda g: g.dma_start(
            out=wdst(wg_bf[0], 8), in_=wslice(wgT, 0, 8)), dma=True)}
    elif "W" in stages and stages.endswith("w3"):
        t_wl = {0: w_load(0)}
    else:
        t_wl = {0: w_load(0), 1: w_load(1)}
    t_sl = {0: sl_load(0, t_scat), 1: sl_load(1, t_scat)}
    if stages == "ABl":
        all_sl = list(t_sl[0]) + list(t_sl[1])
        for cc2 in range(2, NCC):
            ts2 = sl_load(cc2, [all_sl[-1]])
            all_sl += list(ts2)
        return copy_out() + all_sl
    t_tp_done = {}
    t_store = []
    prevB = dict(tr_evac=[None, None], xe_cons=[None, None], g_evac=None,
                 u_evac=None, p_cons=None, down=[None, None],
                 o_evac=[None, None], ost=[None, None])

    ncc_run = NCC
    if stages.startswith("ABtT") or stages.startswith("ABtE"):
        tail = stages[4:].split("W")[0]
        ncc_run = int(tail) if tail else NCC
    elif stages.startswith("ABt") and len(stages) > 3:
        ncc_run = int(stages[3:])
    for cc in range(ncc_run):
        e, ch = cc // NCH, cc % NCH
        xe = xe_fm[cc % 2]
        for j in range(2):
            t_trj = sch.op("T", lambda t, j=j, cc=cc: [
                t.matmul(p_tr[j][:, k * 128:(k + 1) * 128],
                         st_tm[(2 * cc + j) % 4][:, k * 128:(k + 1) * 128],
                         c["ident_bf"][:], start=True, stop=True)
                for k in range(8)][-1],
                deps=t_sl[cc] + [prevB["tr_evac"][j], prevB["o_evac"][j], CT])
            prevB["tr_evac"][j] = sch.op(
                "V", lambda v, j=j, xe=xe: v.tensor_copy(
                    out=xe[:].rearrange("p (k t) -> p k t", k=8)[:, :, j * 128:(j + 1) * 128],
                    in_=p_tr[j][:].rearrange("p (k t) -> p k t", k=8)),
                deps=[t_trj, prevB["xe_cons"][cc % 2] if j == 0 else None])
        t_xe = prevB["tr_evac"][1]
        t_tp_done[cc] = t_xe
        if cc + 2 < NCC:
            t_sl[cc + 2] = sl_load(cc + 2, [t_xe])
        if stages.startswith("ABt"):
            prevB["xe_cons"][cc % 2] = t_xe
            continue
        t_lastgu = None
        t_ps = [None] * 4
        for m in range(4):
            t_gm = sch.op("T", lambda t, m=m, e=e, xe=xe: [
                t.matmul(p_g[m % 2][:],
                         wg_bf[e % 2][:, k * I0 + m * 128:k * I0 + (m + 1) * 128],
                         xe[:, k * 256:(k + 1) * 256],
                         start=(k == 0), stop=(k == 7)) for k in range(8)][-1],
                deps=[t_xe, t_wl.get(e), prevB["g_evac"]])
            t_um = sch.op("T", lambda t, m=m, e=e, xe=xe: [
                t.matmul(p_u[:],
                         wu_bf[e % 2][:, k * I0 + m * 128:k * I0 + (m + 1) * 128],
                         xe[:, k * 256:(k + 1) * 256],
                         start=(k == 0), stop=(k == 7)) for k in range(8)][-1],
                deps=[t_xe, t_wl.get(e), prevB["u_evac"]])
            t_lastgu = t_um
            t_ge = sch.op("S", lambda s, m=m: s.activation(
                out=gbuf[:], in_=p_g[m % 2][:], func=AF.Silu),
                deps=[t_gm, prevB["p_cons"]])
            prevB["g_evac"] = t_ge
            t_ue = sch.op("V", lambda v: v.tensor_copy(
                out=ubuf[:], in_=p_u[:]), deps=[t_um, prevB["p_cons"]])
            prevB["u_evac"] = t_ue
            t_ps[m] = sch.op("V", lambda v, m=m: v.tensor_tensor(
                out=pbuf[m][:], in0=gbuf[:], in1=ubuf[:], op=ALU.mult),
                deps=[t_ge, t_ue, prevB["down"][0]])
            prevB["p_cons"] = t_ps[m]
            if stages == "ABg":
                continue
        if stages == "ABg":
            prevB["xe_cons"][cc % 2] = t_lastgu
            if ch == NCH - 1:
                last_gu_of_expert[e] = t_lastgu
                last_down_of_expert[e] = t_lastgu
                if e + 2 < E0:
                    t_wl[e + 2] = w_load(e + 2)
            continue
        t_dn = sch.op("T", lambda t, e=e: [
            t.matmul(p_o[sm][:, nh * 512:(nh + 1) * 512],
                     pbuf[m][:, sm * 128:(sm + 1) * 128],
                     wd_bf[e % 2][:, m * D + nh * 512:m * D + (nh + 1) * 512],
                     start=(m == 0), stop=(m == 3))
            for sm in range(2) for nh in range(2) for m in range(4)][-1],
            deps=t_ps + [t_wl.get(e), prevB["o_evac"][0], prevB["o_evac"][1]])
        prevB["down"] = [t_dn, t_dn]
        prevB["xe_cons"][cc % 2] = t_lastgu
        if ch == NCH - 1:
            last_gu_of_expert[e] = t_lastgu
            last_down_of_expert[e] = t_dn
            if e + 2 < E0:
                t_wl[e + 2] = w_load(e + 2)
        for sm in range(2):
            t_oe = sch.op("S", lambda s, sm=sm: s.activation(
                out=o_sb[sm][:], in_=p_o[sm][:], func=AF.Copy,
                scale=c["half_col"][:, 0:1]),
                deps=[t_dn, prevB["ost"][sm], CT])
            prevB["o_evac"][sm] = t_oe
            r0 = e * CAP0 + ch * 256 + sm * 128
            t_st = sch.op("Q", lambda q, r0=r0, sm=sm: q.dma_start(
                out=oslots[r0:r0 + 128, :], in_=o_sb[sm][:]),
                deps=[t_oe], dma=True)
            prevB["ost"][sm] = t_st
            t_store.append(t_st)

    if stages.startswith("ABt"):
        toks = copy_out()
        return toks + list(t_tp_done.values())
    if stages == "ABg":
        return copy_out() + [prevB["p_cons"]]
    if stages == "AB":
        toks = copy_out()
        return toks + t_store

    gi32 = [P.sb(f"{tag}_gi{j}", [128, 1], I32) for j in range(2)]
    rs = [P.sb(f"{tag}_rs{j}", [128, D], F32) for j in range(2)]
    got = [P.sb(f"{tag}_got{j}", [128, D], F32) for j in range(2)]
    onew = [P.sb(f"{tag}_onew{j}", [128, D], F32) for j in range(2)]
    t_out = []
    prevC = dict(gath=[None, None], add=[None, None], st=[None, None])
    for i in range(NT):
        jb = i % 2
        t_g32 = sch.op("S", lambda s, i=i, jb=jb: s.activation(
            out=gi32[jb][:], in_=gidx_all[:, i:i + 1], func=AF.Copy),
            deps=[t_gidx[i], prevC["gath"][jb]], flush=True)
        t_ga = sch.op("G", lambda g, jb=jb: g.indirect_dma_start(
            out=got[jb][:], out_offset=None,
            in_=oslots, in_offset=IndirectOffsetOnAxis(gi32[jb][:, 0:1], 0),
            bounds_check=greg(P, g, SLOTS0), oob_is_err=False),
            deps=[t_g32, t_zrow, prevC["add"][jb]] + t_store, dma=True)
        prevC["gath"][jb] = t_ga
        t_rl = sch.op("Q", lambda q, i=i, jb=jb: q.dma_start(
            out=rs[jb][:], in_=src_t[i]), deps=[prevC["add"][jb]], dma=True)
        t_add = sch.op("V", lambda v, jb=jb: v.tensor_tensor(
            out=onew[jb][:], in0=rs[jb][:], in1=got[jb][:], op=ALU.add),
            deps=[t_ga, t_rl, prevC["st"][jb]])
        prevC["add"][jb] = t_add
        t_st = sch.op("Q", lambda q, i=i, jb=jb: q.dma_start(
            out=dst_t[i], in_=onew[jb][:]), deps=[t_add], dma=True)
        prevC["st"][jb] = t_st
        t_out.append(t_st)
    return t_out





def emit_l0_attn(P, sch, src, dst, tag):
    sc1 = contextlib.ExitStack()
    c = P.consts
    src_t = src.rearrange("(n p) d -> n p d", p=128)
    dst_t = dst.rearrange("(n p) d -> n p d", p=128)
    wq, wk, wv = P.dram["l0a_wq"], P.dram["l0a_wk"], P.dram["l0a_wv"]
    wp, bp = P.dram["l0a_wp"], P.dram["l0a_bp"]
    bq, bk, bv = P.dram["l0a_bq"], P.dram["l0a_bk"], P.dram["l0a_bv"]
    attn_dram = P.dram[f"{tag}_attn"]
    CT = P.consts_tok
    FV = P.flushers["V"]

    NB = 3
    x_t = [sbs(P, sc1, f"{tag}_x{j}", [128, D], F32) for j in range(NB)]
    xn_f = sbs(P, sc1, f"{tag}_xn", [128, D], F32)
    xn_bf = sbs(P, sc1, f"{tag}_xnbf", [128, D], BF16)
    w = ln_scratch(P, tag, scope=sc1)
    xn_fm = sbs(P, sc1, f"{tag}_xnfm", [128, 8 * TOK], BF16)
    bq_sb = sbs(P, sc1, f"{tag}_bq", [128, 8], F32)
    bk_sb = sbs(P, sc1, f"{tag}_bk", [128, 8], F32)
    bv_sb = sbs(P, sc1, f"{tag}_bv", [128, 8], F32)

    t_b = sch.op("Q", lambda q: [
        q.dma_start(out=bq_sb[:], in_=bq),
        q.dma_start(out=bk_sb[:], in_=bk),
        q.dma_start(out=bv_sb[:], in_=bv)], dma=True, ndma=3)

    p_tr = P.PB0[:, 0:D]
    t_xn = [None] * NT
    t_fmst = [None] * NT
    prev = dict(tp=None, ev=None)
    for i in range(NT):
        ib = i % NB
        t_xl = sch.op("Q", lambda q, i=i, ib=ib: q.dma_start(
            out=x_t[ib][:], in_=src_t[i]),
            deps=[t_xn[i - NB] if i >= NB else None], dma=True)
        t_xn[i] = emit_ln(sch, w, x_t[ib], t_xl, xn_f,
                          xn_guard=prev["tp"], out_bf=xn_bf)
        t_tp = sch.op("T", lambda t, i=i: [
            t.matmul(p_tr[:, k * 128:(k + 1) * 128],
                     xn_bf[:, k * 128:(k + 1) * 128], c["ident_bf"][:],
                     start=True, stop=True) for k in range(8)][-1],
            deps=[t_xn[i], prev["ev"], CT])
        prev["tp"] = t_tp
        t_ev = sch.op("S", lambda s, i=i: s.activation(
            out=xn_fm[:].rearrange("p (k t) -> p k t", k=8)[:, :, i * 128:(i + 1) * 128],
            in_=p_tr[:].rearrange("p (k t) -> p k t", k=8), func=AF.Copy),
            deps=[t_tp])
        prev["ev"] = t_ev
        t_fmst[i] = t_ev
    t_xn_guard = prev["tp"]
    all_fm = t_fmst

    wq_bf = [sbs(P, sc1, f"{tag}_wq{j}", [128, 8 * 128], BF16) for j in range(2)]
    wk_bf = [sbs(P, sc1, f"{tag}_wk{j}", [128, 8 * 128], BF16) for j in range(2)]
    wv_bf = [sbs(P, sc1, f"{tag}_wv{j}", [128, 8 * 128], BF16) for j in range(2)]
    q_bf = sbs(P, sc1, f"{tag}_qbf", [128, TOK], BF16)
    k_bf = sbs(P, sc1, f"{tag}_kbf", [128, TOK], BF16)
    v_bf = sbs(P, sc1, f"{tag}_vbf", [128, TOK], BF16)
    kT = sbs(P, sc1, f"{tag}_kT", [128, TOK], BF16)
    vT = sbs(P, sc1, f"{tag}_vT", [128, TOK], BF16)
    kv_bf = sbs(P, sc1, f"{tag}_kvbf", [128, 2 * 64], BF16)
    attn_hp = sbs(P, sc1, f"{tag}_attnhp", [128, TOK], BF16)
    tbuf = sbs(P, sc1, f"{tag}_tbuf", [128, 512], F32)
    mbuf = sbs(P, sc1, f"{tag}_mbuf", [128, 512], F32)
    ebuf = sbs(P, sc1, f"{tag}_ebuf", [128, 512], F32)

    p_q, p_k = P.PM0[:, 0:512], P.PM1[:, 0:512]
    p_v = P.PB0[:, 0:512]
    p_t2 = P.PB1[:, 0:D]
    p_kv = {(b, h): P.PB0[h * 64:(h + 1) * 64, 512 + b * 64: 512 + (b + 1) * 64]
            for b in range(2) for h in range(2)}
    p_oh = P.PM0[0:64, 0:512]

    wl_chain = [None]

    def awload(hp):
        d0 = [wl_chain[0], aw_cons.get(hp - 2)]
        ta = sch.op("G", lambda g, hp=hp: g.dma_start(
            out=wq_bf[hp % 2][:].rearrange("p (k m) -> p k m", k=8),
            in_=wq.rearrange("(k p) m -> p k m", p=128)[:, :, hp * 128:(hp + 1) * 128]),
            deps=d0, dma=True)
        tb = sch.op("G", lambda g, hp=hp: g.dma_start(
            out=wk_bf[hp % 2][:].rearrange("p (k m) -> p k m", k=8),
            in_=wk.rearrange("(k p) m -> p k m", p=128)[:, :, hp * 128:(hp + 1) * 128]),
            deps=[ta], dma=True)
        tc = sch.op("G", lambda g, hp=hp: g.dma_start(
            out=wv_bf[hp % 2][:].rearrange("p (k m) -> p k m", k=8),
            in_=wv.rearrange("(k p) m -> p k m", p=128)[:, :, hp * 128:(hp + 1) * 128]),
            deps=[tb], dma=True)
        wl_chain[0] = tc
        return tc

    aw_cons = {}
    t_awl = {0: awload(0), 1: awload(1)}
    t_ast = []
    prevH = dict(qkv_ev=[None, None, None], tpev=None, kvev=None,
                 ohev=None, attn_st=None)

    NCHT = TOK // 512
    for hp in range(8):
        t_lastmm = None
        for ch in range(NCHT):
            cdep = [t_awl[hp]] + (all_fm if (hp == 0 and ch == 0) else [])
            if ch == 0:
                cdep += [prevH["ohev"], prevH["kvev"], prevH["tpev"]]
            t_qm = sch.op("T", lambda t, hp=hp, ch=ch: [
                t.matmul(p_q, wq_bf[hp % 2][:, k * 128:(k + 1) * 128],
                         xn_fm[:].rearrange("p (k t) -> p k t", k=8)[:, k, ch * 512:(ch + 1) * 512],
                         start=(k == 0), stop=(k == 7)) for k in range(8)][-1],
                deps=cdep + [prevH["qkv_ev"][0]])
            t_km = sch.op("T", lambda t, hp=hp, ch=ch: [
                t.matmul(p_k, wk_bf[hp % 2][:, k * 128:(k + 1) * 128],
                         xn_fm[:].rearrange("p (k t) -> p k t", k=8)[:, k, ch * 512:(ch + 1) * 512],
                         start=(k == 0), stop=(k == 7)) for k in range(8)][-1],
                deps=cdep + [prevH["qkv_ev"][1]])
            t_vm = sch.op("T", lambda t, hp=hp, ch=ch: [
                t.matmul(p_v, wv_bf[hp % 2][:, k * 128:(k + 1) * 128],
                         xn_fm[:].rearrange("p (k t) -> p k t", k=8)[:, k, ch * 512:(ch + 1) * 512],
                         start=(k == 0), stop=(k == 7)) for k in range(8)][-1],
                deps=cdep + [prevH["qkv_ev"][2]])
            t_lastmm = t_vm
            def f_q(v, hp=hp, ch=ch, ps=p_q, bias=bq_sb, out=q_bf):
                v.tensor_scalar(out=tbuf[:], in0=ps, scalar1=bias[:, hp:hp + 1],
                                scalar2=None, op0=ALU.add)
                return v.tensor_scalar(out=mbuf[:], in0=tbuf[:], scalar1=0.0,
                                       scalar2=None, op0=ALU.min)
            t_q1 = sch.op("V", f_q, deps=[t_qm, prevH["qkv_ev"][0], t_b])
            t_q2 = sch.op("S", lambda s: s.activation(
                out=ebuf[:], in_=mbuf[:], func=AF.Exp), deps=[t_q1])
            t_q3 = sch.op("V", lambda v, ch=ch: v.scalar_tensor_tensor(
                out=q_bf[:, ch * 512:(ch + 1) * 512], in0=tbuf[:], scalar=0.0,
                in1=ebuf[:], op0=ALU.max, op1=ALU.add),
                deps=[t_q2] + ([prevH["ohev"]] if ch == 0 else []))
            def f_k(v, hp=hp, ch=ch):
                v.tensor_scalar(out=tbuf[:], in0=p_k, scalar1=bk_sb[:, hp:hp + 1],
                                scalar2=None, op0=ALU.add)
                return v.tensor_scalar(out=mbuf[:], in0=tbuf[:], scalar1=0.0,
                                       scalar2=None, op0=ALU.min)
            t_k1 = sch.op("V", f_k, deps=[t_km, t_q3, prevH["qkv_ev"][1], t_b])
            t_k2 = sch.op("S", lambda s: s.activation(
                out=ebuf[:], in_=mbuf[:], func=AF.Exp), deps=[t_k1])
            t_k3 = sch.op("V", lambda v, ch=ch: v.scalar_tensor_tensor(
                out=k_bf[:, ch * 512:(ch + 1) * 512], in0=tbuf[:], scalar=0.0,
                in1=ebuf[:], op0=ALU.max, op1=ALU.add),
                deps=[t_k2] + ([prevH["tpev"]] if ch == 0 else []))
            t_v1 = sch.op("V", lambda v, hp=hp, ch=ch: v.tensor_scalar(
                out=v_bf[:, ch * 512:(ch + 1) * 512], in0=p_v,
                scalar1=bv_sb[:, hp:hp + 1], scalar2=None, op0=ALU.add),
                deps=[t_vm, prevH["qkv_ev"][2], t_b] +
                     ([prevH["tpev"]] if ch == 0 else []))
            prevH["qkv_ev"] = [t_q3, t_k3, t_v1]
        aw_cons[hp] = t_lastmm
        if hp + 2 < 8:
            t_awl[hp + 2] = awload(hp + 2)
        t_qkv = prevH["qkv_ev"]

        t_tps = []
        for half in range(2):
            for src_buf, dst_buf in ((k_bf, kT), (v_bf, vT)):
                t_tp = sch.op("T", lambda t, half=half, sb_=src_buf: [
                    t.matmul(p_t2[:, j * 128:(j + 1) * 128],
                             sb_[:, (half * 8 + j) * 128:(half * 8 + j + 1) * 128],
                             c["ident_bf"][:], start=True, stop=True)
                    for j in range(8)][-1],
                    deps=t_qkv + [prevH["tpev"], CT])
                prevH["tpev"] = sch.op("S", lambda s, half=half, db=dst_buf: s.activation(
                    out=db[:, half * 1024:(half + 1) * 1024], in_=p_t2[:],
                    func=AF.Copy), deps=[t_tp])
                t_tps.append(prevH["tpev"])
        for half in range(2):
            for src_buf, dst_buf in ((k_bf, kT), (v_bf, vT)):
                t_tp = sch.op("T", lambda t, half=half, sb_=src_buf: [
                    t.matmul(p_t2[:, j * 128:(j + 1) * 128],
                             sb_[:, (2048 + (half * 8 + j) * 128):(2048 + (half * 8 + j + 1) * 128)],
                             c["ident_bf"][:], start=True, stop=True)
                    for j in range(8)][-1],
                    deps=t_qkv + [prevH["tpev"], CT])
                prevH["tpev"] = sch.op("S", lambda s, half=half, db=dst_buf: s.activation(
                    out=db[:, 2048 + half * 1024:2048 + (half + 1) * 1024], in_=p_t2[:],
                    func=AF.Copy), deps=[t_tp])
                t_tps.append(prevH["tpev"])

        t_kvm = sch.op("T", lambda t: [
            t.matmul(p_kv[(b, h)],
                     kT[:, (b * 16 + tt) * 128 + h * 64:(b * 16 + tt) * 128 + h * 64 + 64],
                     vT[:, (b * 16 + tt) * 128 + h * 64:(b * 16 + tt) * 128 + h * 64 + 64],
                     start=(tt == 0), stop=(tt == 15),
                     tile_position=(0, h * 64))
            for b in range(2) for h in range(2) for tt in range(16)][-1],
            deps=t_tps + [prevH["kvev"]])
        t_kve = sch.op("S", lambda s: [
            s.activation(out=kv_bf[h * 64:(h + 1) * 64, b * 64:(b + 1) * 64],
                         in_=p_kv[(b, h)], func=AF.Copy)
            for b in range(2) for h in range(2)][-1],
            deps=[t_kvm, prevH["ohev"], prevH["attn_st"]], flush=True)
        prevH["kvev"] = t_kve

        t_oh_all = []
        for b in range(2):
            for h in range(2):
                for ch in range(4):
                    t_ohm = sch.op("T", lambda t, b=b, h=h, ch=ch: t.matmul(
                        p_oh, kv_bf[h * 64:(h + 1) * 64, b * 64:(b + 1) * 64],
                        q_bf[h * 64:(h + 1) * 64, b * 2048 + ch * 512: b * 2048 + (ch + 1) * 512],
                        start=True, stop=True),
                        deps=[t_kve, prevH["ohev"]])
                    prevH["ohev"] = sch.op("S", lambda s, b=b, h=h, ch=ch: s.activation(
                        out=attn_hp[h * 64:(h + 1) * 64,
                                    b * 2048 + ch * 512: b * 2048 + (ch + 1) * 512],
                        in_=p_oh, func=AF.Copy), deps=[t_ohm])
                    t_oh_all.append(prevH["ohev"])
        t_st = sch.op("Q", lambda q, hp=hp: q.dma_start(
            out=attn_dram[:, hp, :], in_=attn_hp[:]),
            deps=t_oh_all + [prevH["attn_st"]], dma=True)
        prevH["attn_st"] = t_st
        t_ast.append(t_st)

    sc1.close()
    wp_bf = P.sb(f"{tag}_wpbf", [128, 8 * D], BF16)
    bp_bf = P.sb(f"{tag}_bpbf", [1, D], BF16)
    aslab = [P.sb(f"{tag}_aslab{j}", [128, 8 * 1024], BF16) for j in range(2)]
    onew = [P.sb(f"{tag}_onew{j}", [128, D], F32) for j in range(2)]
    p_wp = [P.PB0[:, 0:512], P.PB1[:, 0:512]]

    t_wp = sch.op("G", lambda g: g.dma_start(
        out=wp_bf[:].rearrange("p (k m) -> p k m", k=8),
        in_=wp.rearrange("(k p) m -> p k m", p=128)),
        deps=[wl_chain[0], t_ast[-1]], dma=True)
    t_bp = sch.op("G", lambda g: g.dma_start(out=bp_bf[:], in_=bp),
                  deps=[t_wp], dma=True)
    t_out = []
    prevW = dict(mm=[None, None], st=[None, None])
    t_sl = {}
    slab_cons = {}

    def slab_load(C, dep):
        return sch.op("Q", lambda q, C=C: q.dma_start(
            out=aslab[C % 2][:].rearrange("p (k t) -> p k t", k=8),
            in_=attn_dram[:, :, C * 1024:(C + 1) * 1024]),
            deps=[t_ast[-1]] + dep, dma=True)

    t_sl[0] = slab_load(0, [])
    t_sl[1] = slab_load(1, [])
    rs_t = [P.sb(f"{tag}_rst{j}", [128, D], F32) for j in range(3)]
    t_rlv = [None] * NT
    t_adv = [None] * NT

    def rs_load(i, dep):
        return sch.op("Q", lambda q, i=i: q.dma_start(
            out=rs_t[i % 3][:], in_=src_t[i]),
            deps=[t_ast[-1], dep], dma=True)

    for i in range(3):
        t_rlv[i] = rs_load(i, None)
    for i in range(NT):
        C, m = i // 8, i % 8
        jb = C % 2
        mm = []
        for nh in range(2):
            t_mm = sch.op("T", lambda t, m=m, nh=nh, jb=jb: [
                t.matmul(p_wp[jb],
                         aslab[jb][:].rearrange("p (k t) -> p k t", k=8)[:, k, m * 128:(m + 1) * 128],
                         wp_bf[:, k * D + nh * 512:k * D + nh * 512 + 512],
                         start=(k == 0), stop=False) for k in range(8)] and
                t.matmul(p_wp[jb], c["ones1_bf"][:],
                         bp_bf[:, nh * 512:nh * 512 + 512],
                         start=False, stop=True),
                deps=[t_sl[C], t_wp, t_bp, prevW["mm"][jb], CT, t_ast[-1]])
            t_ad = sch.op("V", lambda v, nh=nh, jb=jb, i=i: v.tensor_tensor(
                out=onew[jb][:, nh * 512:nh * 512 + 512],
                in0=rs_t[i % 3][:, nh * 512:nh * 512 + 512],
                in1=p_wp[jb], op=ALU.add),
                deps=[t_mm, t_rlv[i], prevW["st"][jb]])
            mm.append(t_ad)
            prevW["mm"][jb] = t_ad
        t_adv[i] = mm[-1]
        slab_cons.setdefault(C, []).append(mm[-1])
        if i + 3 < NT:
            t_rlv[i + 3] = rs_load(i + 3, t_adv[i])
        if m == 0 and C >= 1 and C + 1 <= 3:
            t_sl[C + 1] = slab_load(C + 1, slab_cons.get(C - 1, []))
        t_st2 = sch.op("Q", lambda q, i=i, jb=jb: q.dma_start(
            out=dst_t[i], in_=onew[jb][:]), deps=mm, dma=True)
        prevW["st"][jb] = t_st2
        t_out.append(t_st2)
    return t_out





def emit_l1_ffn(P, sch, src, dst, tag):
    sc1 = contextlib.ExitStack()
    c = P.consts
    FV = P.flushers["V"]
    CT = P.consts_tok
    src_t = src.rearrange("(n p) d -> n p d", p=128)
    dst_t = dst.rearrange("(n p) d -> n p d", p=128)
    xnorm = P.dram[f"{tag}_xnorm"]
    comb = P.dram[f"{tag}_comb"]
    rw1, rw2 = P.dram["l1_rw1"], P.dram["l1_rw2"]
    w1d, w2d, w3d = P.dram["l1_w1"], P.dram["l1_w2"], P.dram["l1_w3"]

    LG = P.sb(f"{tag}_LG", [64, T], F32)
    rank_tm = P.sb(f"{tag}_ranktm", [128, NTB * 64], F32)
    sel_tm = P.sb(f"{tag}_seltm", [128, NTB * 64], F32)
    W_tm = P.sb(f"{tag}_Wtm", [128, NTB * 64], F32)
    rhs_all = P.sb(f"{tag}_rhsall", [128, NTB * 66], BF16)
    iotap_bf = P.sb(f"{tag}_iotapbf", [128, 1], BF16)
    iota16_bf = P.sb(f"{tag}_iota16bf", [128, 16], BF16)
    idx_i32 = P.sb(f"{tag}_idx", [128, 2 * E1 * 2], I32)
    wcol = P.sb(f"{tag}_wcol", [128, 2 * E1 * 2], F32)
    P._l1dbg = dict(idx=idx_i32, wcol=wcol, LG=LG, rank_tm=rank_tm)

    NB = 3
    x_t = [sbs(P, sc1, f"{tag}_x{j}", [128, D], F32) for j in range(NB)]
    xn_f = sbs(P, sc1, f"{tag}_xn", [128, D], F32)
    xn_bf = [sbs(P, sc1, f"{tag}_xnbf{j}", [128, D], BF16) for j in range(2)]
    w = ln_scratch(P, tag, scope=sc1)
    fm = sbs(P, sc1, f"{tag}_fm", [128, D], F32)
    rw1_sb = sbs(P, sc1, f"{tag}_rw1", [128, 8 * 128], F32)
    rw2_sb = sbs(P, sc1, f"{tag}_rw2", [128, 16], F32)
    z1_sb = sbs(P, sc1, f"{tag}_z1", [128, 128], F32)
    z1f = sbs(P, sc1, f"{tag}_z1f", [128, 128], F32)

    p_tp = P.PB0[:, 0:D]
    p_z = P.PM0[:, 0:128]
    p_zT = P.PM1[:, 0:128]
    p_lg = P.PM0[0:16, 256:384]

    t_rw = sch.op("Q", lambda q: [
        q.dma_start(out=rw1_sb[:].rearrange("p (k m) -> p k m", k=8),
                    in_=rw1.rearrange("(k p) m -> p k m", p=128)),
        q.dma_start(out=rw2_sb[:], in_=rw2)], dma=True, ndma=2)
    t_lg0 = sch.op("V", lambda v: v.memset(LG[:], -1e30), deps=[CT], flush=True)

    t_xn = [None] * NT
    t_xst = [None] * NT
    prev1 = dict(tp=None, fmev=None, z1mm=None, z1ev=None, zTmm=None,
                 zTev=None, lgmm=None, lgev=t_lg0)
    for i in range(NT):
        ib = i % NB
        t_xl = sch.op("Q", lambda q, i=i, ib=ib: q.dma_start(
            out=x_t[ib][:], in_=src_t[i]),
            deps=[t_xn[i - NB] if i >= NB else None], dma=True)
        t_xn[i] = emit_ln(sch, w, x_t[ib], t_xl, xn_f,
                          xn_guard=prev1["tp"], out_bf=xn_bf[i % 2])
        t_xst[i] = sch.op("Q", lambda q, i=i, ib2=i % 2: q.dma_start(
            out=xnorm.rearrange("(n p) d -> n p d", p=128)[i],
            in_=xn_bf[ib2][:]),
            deps=[t_xn[i], t_xst[i - 2] if i >= 2 else None], dma=True)
        t_tp = sch.op("T", lambda t, i=i: [
            t.matmul(p_tp[:, k * 128:(k + 1) * 128],
                     xn_f[:, k * 128:(k + 1) * 128], c["ident_f32"][:],
                     start=True, stop=True) for k in range(8)][-1],
            deps=[t_xn[i], prev1["fmev"], CT])
        prev1["tp"] = t_tp
        t_fm = sch.op("S", lambda s: s.activation(
            out=fm[:], in_=p_tp[:], func=AF.Copy), deps=[t_tp, prev1["z1mm"]])
        prev1["fmev"] = t_fm
        t_z1m = sch.op("T", lambda t: [
            t.matmul(p_z, fm[:, k * 128:(k + 1) * 128],
                     rw1_sb[:, k * 128:(k + 1) * 128],
                     start=(k == 0), stop=(k == 7)) for k in range(8)][-1],
            deps=[t_fm, t_rw, prev1["z1ev"]])
        prev1["z1mm"] = t_z1m
        t_z1e = sch.op("S", lambda s: s.activation(
            out=z1_sb[:], in_=p_z, func=AF.Silu), deps=[t_z1m, prev1["zTmm"]])
        prev1["z1ev"] = t_z1e
        t_zTm = sch.op("T", lambda t: t.matmul(
            p_zT, z1_sb[:], c["ident_f32"][:], start=True, stop=True),
            deps=[t_z1e, prev1["zTev"]])
        prev1["zTmm"] = t_zTm
        t_zTe = sch.op("S", lambda s: s.activation(
            out=z1f[:], in_=p_zT, func=AF.Copy), deps=[t_zTm, prev1["lgmm"]])
        prev1["zTev"] = t_zTe
        t_lgm = sch.op("T", lambda t: t.matmul(
            p_lg, rw2_sb[:], z1f[:], start=True, stop=True),
            deps=[t_zTe, t_rw, prev1["lgev"]])
        prev1["lgmm"] = t_lgm
        rb = 32 * (i // NTB)
        t_lge = sch.op("S", lambda s, i=i, rb=rb: s.activation(
            out=LG[rb:rb + 16, (i % NTB) * 128:(i % NTB + 1) * 128],
            in_=p_lg, func=AF.Copy), deps=[t_lgm])
        prev1["lgev"] = t_lge
    t_lg_done = prev1["lgev"]

    sc1.close()
    sc2 = contextlib.ExitStack()
    Pex = sbs(P, sc2, f"{tag}_Pex", [64, T], F32)
    junk = sbs(P, sc2, f"{tag}_junk", [64, T], F32)
    PM = sbs(P, sc2, f"{tag}_PM", [64, T], F32)
    Wrow = sbs(P, sc2, f"{tag}_W", [64, T], F32)
    selr = sbs(P, sc2, f"{tag}_sel", [64, T], F32)
    pcA = sbs(P, sc2, f"{tag}_pcA", [64, T], F32)
    pcB = sbs(P, sc2, f"{tag}_pcB", [64, T], F32)
    nmax = sbs(P, sc2, f"{tag}_nmax", [64, 1], F32)
    Zs = sbs(P, sc2, f"{tag}_Z", [64, 1], F32)
    lo = sbs(P, sc2, f"{tag}_lo", [64, 1], F32)
    hi = sbs(P, sc2, f"{tag}_hi", [64, 1], F32)
    mid = sbs(P, sc2, f"{tag}_mid", [64, 1], F32)
    cnt = sbs(P, sc2, f"{tag}_cnt", [64, 1], F32)
    geb = sbs(P, sc2, f"{tag}_ge", [64, 1], U8)
    stp = sbs(P, sc2, f"{tag}_stp", [64, 1], F32)
    den = sbs(P, sc2, f"{tag}_den", [64, 1], F32)
    rec = sbs(P, sc2, f"{tag}_rec", [64, 1], F32)

    def f_soft(v):
        v.reduce_max(out=hi[:], in_=LG[:], axis=AX.X)
        FV(v)
        return v.tensor_scalar(out=nmax[:], in0=hi[:], scalar1=-1.0,
                               scalar2=None, op0=ALU.mult)
    t_sm = sch.op("V", f_soft, deps=[t_lg_done], flush=True)
    t_p = sch.op("S", lambda s: s.activation(
        out=Pex[:], in_=LG[:], func=AF.Exp, bias=nmax[:, 0:1]),
        deps=[t_sm], flush=True)

    def f_z(v):
        v.reduce_sum(out=Zs[:], in_=Pex[:], axis=AX.X)
        v.tensor_reduce(out=lo[:], in_=LG[:], axis=AX.X, op=ALU.min)
        FV(v)
        return v.tensor_scalar(out=lo[:], in0=lo[:], scalar1=1.0,
                               scalar2=None, op0=ALU.subtract)
    t_z = sch.op("V", f_z, deps=[t_p], flush=True)

    t_prev = t_z
    for it in range(40):
        def f_iter(v, it=it):
            v.tensor_tensor(out=mid[:], in0=lo[:], in1=hi[:], op=ALU.add)
            FV(v)
            v.tensor_scalar(out=mid[:], in0=mid[:], scalar1=0.5, scalar2=None,
                            op0=ALU.mult)
            FV(v)
            v.tensor_scalar(out=junk[:], in0=LG[:], scalar1=mid[:, 0:1],
                            scalar2=None, op0=ALU.is_gt, op1=ALU.add,
                            accum_out=cnt[:, 0:1])
            FV(v)
            v.tensor_scalar(out=geb[:], in0=cnt[:], scalar1=256.0,
                            scalar2=None, op0=ALU.is_ge)
            FV(v)
            v.select(out=lo[:], mask=geb[:], on_true=mid[:], on_false=lo[:])
            return v.select(out=hi[:], mask=geb[:], on_true=hi[:],
                            on_false=mid[:])
        t_prev = sch.op("V", f_iter, deps=[t_prev], flush=True)

    def f_wsel(v):
        v.scalar_tensor_tensor(out=PM[:], in0=LG[:], scalar=lo[:, 0:1],
                               in1=Pex[:], op0=ALU.is_gt, op1=ALU.mult,
                               accum_out=stp[:, 0:1])
        FV(v)
        v.tensor_scalar(out=den[:], in0=Zs[:], scalar1=1e-9, scalar2=None,
                        op0=ALU.mult)
        FV(v)
        v.tensor_tensor(out=den[:], in0=stp[:], in1=den[:], op=ALU.add)
        FV(v)
        v.reciprocal(out=rec[:], in_=den[:])
        FV(v)
        v.tensor_scalar(out=Wrow[:], in0=PM[:], scalar1=rec[:, 0:1],
                        scalar2=0.5, op0=ALU.mult, op1=ALU.mult)
        return v.tensor_scalar(out=selr[:], in0=PM[:], scalar1=0.0,
                               scalar2=None, op0=ALU.is_gt)
    t_wsel = sch.op("V", f_wsel, deps=[t_prev], flush=True)

    t_pc = t_wsel
    bufs = [selr, pcA, pcB]
    cur = 0
    for lev, sh in enumerate([1, 2, 4, 8, 16, 32, 64, 128, 256, 512, 1024]):
        a = bufs[cur]
        b = bufs[1 + (lev % 2)]
        def f_shift(v, a=a, b=b, sh=sh):
            v.tensor_copy(out=b[:, 0:sh], in_=a[:, 0:sh])
            return v.tensor_tensor(out=b[:, sh:T], in0=a[:, sh:T],
                                   in1=a[:, 0:T - sh], op=ALU.add)
        t_pc = sch.op("V", f_shift, deps=[t_pc])
        cur = 1 + (lev % 2)
    t_rank = sch.op("V", lambda v: v.tensor_tensor(
        out=junk[:], in0=bufs[cur][:], in1=selr[:], op=ALU.subtract),
        deps=[t_pc], flush=True)
    rankr = junk

    p_t3 = P.PM1[:, 128:192]
    t_tms = []
    prev_t3 = None
    for tt in range(NTB):
        for mat, dstb in ((rankr, rank_tm), (selr, sel_tm), (Wrow, W_tm)):
            t_t3 = sch.op("T", lambda t, tt=tt, mat=mat: t.matmul(
                p_t3, mat[0:64, tt * 128:(tt + 1) * 128],
                c["ident_f32"][0:64, 0:64], start=True, stop=True),
                deps=[t_rank, prev_t3, CT])
            prev_t3 = sch.op("S", lambda s, tt=tt, dstb=dstb: s.activation(
                out=dstb[:, tt * 64:(tt + 1) * 64], in_=p_t3, func=AF.Copy),
                deps=[t_t3])
            t_tms.append(prev_t3)

    t_ibf = sch.op("V", lambda v: [
        v.tensor_copy(out=iotap_bf[:], in_=c["iotap"][:]),
        v.tensor_copy(out=iota16_bf[:], in_=c["iota16"][:])][-1],
        deps=[CT], flush=True)
    t_rhs = []
    for tt in range(NTB):
        def f_rhs(v, tt=tt):
            v.tensor_copy(out=rhs_all[:, tt * 66:tt * 66 + 1], in_=iotap_bf[:])
            v.tensor_copy(out=rhs_all[:, tt * 66 + 1:tt * 66 + 2],
                          in_=iota16_bf[:, tt:tt + 1])
            return v.tensor_copy(out=rhs_all[:, tt * 66 + 2:(tt + 1) * 66],
                                 in_=W_tm[:, tt * 64:(tt + 1) * 64])
        t_rhs.append(sch.op("V", f_rhs, deps=t_tms + [t_ibf],
                            flush=(tt == NTB - 1)))

    sc2.close()
    NEB = E1 * 2
    gts = [[P.sb(f"{tag}_gt{j}_{k}", [128, 128], BF16) for k in range(2)]
           for j in range(2)]
    iebuf = [P.sb(f"{tag}_iebuf{j}", [128, 2], F32) for j in range(2)]
    idxw = [P.sb(f"{tag}_idxw{j}", [128, 2], F32) for j in range(2)]
    p_ie = [P.PM1[:, 256:322], P.PM1[:, 322:388]]
    xe_tm = [P.sb(f"{tag}_xetm{j}", [128, D], BF16) for j in range(4)]
    t_xz = sch.op("V", lambda v: [v.memset(xe_tm[j][:], 0.0)
                                  for j in range(4)][-1],
                  deps=[t_rhs[-1]], flush=True)
    xe_fm = [P.sb(f"{tag}_xefm{j}", [128, 8 * 256], BF16) for j in range(2)]
    w1_bf = [P.sb(f"{tag}_w1b{j}", [128, 8 * 1024], BF16) for j in range(2)]
    w2_bf = [P.sb(f"{tag}_w2b{j}", [128, 8 * 1024], BF16) for j in range(2)]
    w3_bf = [P.sb(f"{tag}_w3b{j}", [128, 8 * 1024], BF16) for j in range(2)]
    gbuf = P.sb(f"{tag}_gbuf", [128, 256], BF16)
    ubuf = P.sb(f"{tag}_ubuf", [128, 256], BF16)
    pbuf = [P.sb(f"{tag}_pbuf{j}", [128, 256], BF16) for j in range(8)]
    o_sb = [P.sb(f"{tag}_osb{j}", [128, D], F32) for j in range(2)]
    zf32 = P.sb(f"{tag}_zf32", [128, D], F32)

    p_g = [P.PM0[:, 0:256], P.PM0[:, 256:512]]
    p_u = P.PM1[:, 0:256]
    p_o = [P.PB0[:, 0:D], P.PB1[:, 0:D]]
    p_tr = [P.PB0[:, 0:D], P.PB1[:, 0:D]]

    t_zf = sch.op("V", lambda v: v.memset(zf32[:], 0.0),
                  deps=[CT, t_rhs[-1]], flush=True)
    t_cz = []
    prev_cz = None
    for zi in range(NT):
        prev_cz = sch.op("Q", lambda q, zi=zi: q.dma_start(
            out=comb.rearrange("(n p) d -> n p d", p=128)[zi], in_=zf32[:]),
            deps=[t_zf, prev_cz], dma=True)
        t_cz.append(prev_cz)

    wl_chain = [t_rhs[-1]]
    whalf_cons = {}

    def l1_wload(e):
        out = []
        for h in range(2):
            dep0 = [wl_chain[0], whalf_cons.get((e - 1, h))]
            ta = sch.op("G", lambda g, e=e, h=h: g.dma_start(
                out=w1_bf[h][:].rearrange("p (k m) -> p k m", k=8),
                in_=w1d[e].rearrange("(k p) m -> p k m", p=128)
                    [:, :, h * 1024:(h + 1) * 1024]), deps=dep0, dma=True)
            tb = sch.op("G", lambda g, e=e, h=h: g.dma_start(
                out=w2_bf[h][:].rearrange("p (k m) -> p k m", k=8),
                in_=w2d[e].rearrange("(k p) m -> p k m", p=128)
                    [:, :, h * 1024:(h + 1) * 1024]), deps=[ta], dma=True)
            tc = sch.op("G", lambda g, e=e, h=h: g.dma_start(
                out=w3_bf[h][:].rearrange("p (k m) -> p k m", k=8),
                in_=w3d[e].rearrange("(k p) m -> p k m", p=128)
                    [:, h * 8:(h + 1) * 8, :]), deps=[tb], dma=True)
            wl_chain[0] = tc
            out.append(tc)
        return out

    t_wl = {0: l1_wload(0)}
    t_scat_chain = [t_cz[-1]]
    all_scats = []
    prevE = dict(gt=[None, None], iev=[None, None], i32=None,
                 gat=[None, None, None, None], trev=[None, None],
                 xe_cons=[None, None], g_evac=None, u_evac=None, p_cons=None,
                 down=[None, None], o_evac=[None, None], osb_cons=[None, None])
    t_idx_done = {}
    t_gat_done = {}
    lastgu_h = {}

    def emit_extract_gather(be):
        b = be % 2
        j = b * 32 + be // 2
        t_ms = []
        for mt in range(2):
            t_gv = [None] * NTB
            t_mm = [None] * NTB

            def build(tt, mt=mt, j=j):
                return sch.op("V", lambda v, j=j, mt=mt, tt=tt:
                    v.tensor_scalar(
                        out=gts[mt][tt % 2][:],
                        in0=c["iota256"][:, mt * 128:(mt + 1) * 128],
                        scalar1=rank_tm[:, tt * 64 + j:tt * 64 + j + 1],
                        scalar2=sel_tm[:, tt * 64 + j:tt * 64 + j + 1],
                        op0=ALU.is_equal, op1=ALU.mult),
                    deps=t_rhs + [t_rank, prevE["gt"][mt],
                                  t_mm[tt - 2] if tt >= 2 else None],
                    flush=(tt == NTB - 1))

            t_gv[0] = build(0)
            for tt in range(NTB):
                if tt + 1 < NTB:
                    t_gv[tt + 1] = build(tt + 1)
                t_mm[tt] = sch.op("T", lambda t, mt=mt, tt=tt: t.matmul(
                    p_ie[mt], gts[mt][tt % 2][:],
                    rhs_all[:, tt * 66:(tt + 1) * 66],
                    start=(tt == 0), stop=(tt == NTB - 1)),
                    deps=[t_gv[min(tt + 1, NTB - 1)], prevE["iev"][mt]])
            prevE["gt"][mt] = t_mm[NTB - 1]
            t_ms.append(t_mm[NTB - 1])
        def f_iev(v, be=be, j=j):
            r = None
            for mt in range(2):
                v.tensor_copy(out=iebuf[mt][:], in_=p_ie[mt][:, 0:2])
                v.tensor_copy(out=wcol[:, be * 2 + mt:be * 2 + mt + 1],
                              in_=p_ie[mt][:, 2 + j:3 + j])
            FV(v)
            for mt in range(2):
                v.scalar_tensor_tensor(
                    out=idxw[be % 2][:, mt:mt + 1], in0=iebuf[mt][:, 1:2],
                    scalar=128.0, in1=iebuf[mt][:, 0:1],
                    op0=ALU.mult, op1=ALU.add)
            FV(v)
            return v.tensor_scalar(
                out=idxw[be % 2][:], in0=idxw[be % 2][:],
                scalar1=float((be % 2) * T), scalar2=None, op0=ALU.add)
        t_iev = sch.op("V", f_iev, deps=t_ms + [prevE["i32"]], flush=True)
        prevE["iev"] = [t_iev, t_iev]
        t_i32 = sch.op("S", lambda s, be=be: s.activation(
            out=idx_i32[:, be * 2:be * 2 + 2], in_=idxw[be % 2][:],
            func=AF.Copy), deps=[t_iev], flush=True)
        prevE["i32"] = t_i32
        t_idx_done[be] = t_i32
        b_ = be % 2
        gats = []
        for mt in range(2):
            t_ga = sch.op("G", lambda g, b_=b_, be=be, mt=mt:
                g.indirect_dma_start(
                    out=xe_tm[(be * 2 + mt) % 4][:], out_offset=None,
                    in_=xnorm,
                    in_offset=IndirectOffsetOnAxis(
                        idx_i32[:, be * 2 + mt:be * 2 + mt + 1], 0),
                    bounds_check=greg(P, g, TOK - 1), oob_is_err=False),
                deps=[t_i32, prevE["gat"][(be * 2 + mt) % 4] or t_xz] + t_xst,
                dma=True)
            gats.append(t_ga)
        t_gat_done[be] = gats

    emit_extract_gather(0)
    emit_extract_gather(1)

    for be in range(NEB):
        b, e = be % 2, be // 2
        xe = xe_fm[be % 2]
        for mt in range(2):
            t_tr = sch.op("T", lambda t, be=be, mt=mt: [
                t.matmul(p_tr[mt][:, k * 128:(k + 1) * 128],
                         xe_tm[(be * 2 + mt) % 4][:, k * 128:(k + 1) * 128],
                         c["ident_bf"][:], start=True, stop=True)
                for k in range(8)][-1],
                deps=[t_gat_done[be][mt], prevE["trev"][mt],
                      prevE["o_evac"][mt], CT])
            prevE["trev"][mt] = sch.op(
                "V", lambda v, mt=mt, xe=xe: v.tensor_copy(
                    out=xe[:].rearrange("p (k t) -> p k t", k=8)
                        [:, :, mt * 128:(mt + 1) * 128],
                    in_=p_tr[mt][:].rearrange("p (k t) -> p k t", k=8)),
                deps=[t_tr, prevE["xe_cons"][be % 2] if mt == 0 else None])
            prevE["gat"][(be * 2 + mt) % 4] = prevE["trev"][mt]
        t_xe = prevE["trev"][1]
        if be + 2 < NEB:
            emit_extract_gather(be + 2)
        t_lastgu = None
        for h in range(2):
            t_ps = [None] * 8
            for m in range(8):
                t_gm = sch.op("T", lambda t, m=m, h=h, xe=xe: [
                    t.matmul(p_g[m % 2][:],
                             w1_bf[h][:, k * 1024 + m * 128:k * 1024 + (m + 1) * 128],
                             xe[:].rearrange("p (k t) -> p k t", k=8)[:, k, :],
                             start=(k == 0), stop=(k == 7)) for k in range(8)][-1],
                    deps=[t_xe, t_wl[e][h], prevE["g_evac"]])
                t_um = sch.op("T", lambda t, m=m, h=h, xe=xe: [
                    t.matmul(p_u[:],
                             w2_bf[h][:, k * 1024 + m * 128:k * 1024 + (m + 1) * 128],
                             xe[:].rearrange("p (k t) -> p k t", k=8)[:, k, :],
                             start=(k == 0), stop=(k == 7)) for k in range(8)][-1],
                    deps=[t_xe, t_wl[e][h], prevE["u_evac"]])
                t_lastgu = t_um
                t_ge = sch.op("S", lambda s, m=m: s.activation(
                    out=gbuf[:], in_=p_g[m % 2][:], func=AF.Silu),
                    deps=[t_gm, prevE["p_cons"]])
                prevE["g_evac"] = t_ge
                t_ue = sch.op("V", lambda v: v.tensor_copy(
                    out=ubuf[:], in_=p_u[:]), deps=[t_um, prevE["p_cons"]])
                prevE["u_evac"] = t_ue
                t_ps[m] = sch.op("V", lambda v, m=m: v.tensor_tensor(
                    out=pbuf[m][:], in0=gbuf[:], in1=ubuf[:], op=ALU.mult),
                    deps=[t_ge, t_ue, prevE["down"][0]])
                prevE["p_cons"] = t_ps[m]
            lastgu_h[(be, h)] = t_lastgu
            if b == 1:
                whalf_cons[(e, h)] = t_lastgu
            t_dn = sch.op("T", lambda t, h=h: [
                t.matmul(p_o[sm][:, nh * 512:(nh + 1) * 512],
                         pbuf[m][:, sm * 128:(sm + 1) * 128],
                         w3_bf[h][:].rearrange("p (k m) -> p k m", k=8)
                             [:, m, nh * 512:(nh + 1) * 512],
                         start=(m == 0), stop=(m == 7))
                for sm in range(2) for nh in range(2) for m in range(8)][-1],
                deps=t_ps + [t_wl[e][h], prevE["o_evac"][0],
                             prevE["o_evac"][1]])
            prevE["down"] = [t_dn, t_dn]
            for sm in range(2):
                t_oe = sch.op("V", lambda v, sm=sm, be=be: v.tensor_scalar(
                    out=o_sb[sm][:], in0=p_o[sm][:],
                    scalar1=wcol[:, be * 2 + sm:be * 2 + sm + 1],
                    scalar2=None, op0=ALU.mult),
                    deps=[t_dn, prevE["osb_cons"][sm], t_idx_done[be]],
                    flush=True)
                prevE["o_evac"][sm] = t_oe
                t_sc = sch.op("G", lambda g, b=b, be=be, sm=sm:
                    g.indirect_dma_start(
                        out=comb,
                        out_offset=IndirectOffsetOnAxis(
                            idx_i32[:, be * 2 + sm:be * 2 + sm + 1], 0),
                        in_=o_sb[sm][:], in_offset=None,
                        bounds_check=greg(P, g, TOK - 1), oob_is_err=False,
                        compute_op=ALU.add),
                    deps=[t_oe, t_scat_chain[-1], t_idx_done[be]], dma=True)
                t_scat_chain.append(t_sc)
                prevE["osb_cons"][sm] = t_sc
                all_scats.append(t_sc)
        prevE["xe_cons"][be % 2] = t_lastgu
        if b == 1 and e + 1 < E1:
            t_wl[e + 1] = l1_wload(e + 1)

    cbt = [P.sb(f"{tag}_cbt{j}", [128, D], F32) for j in range(3)]
    rs_t = [P.sb(f"{tag}_rst{j}", [128, D], F32) for j in range(3)]
    onew = [P.sb(f"{tag}_onew{j}", [128, D], F32) for j in range(2)]
    t_out = []
    t_cl = [None] * NT
    t_rl = [None] * NT
    t_ad = [None] * NT
    prevF = dict(st=[None, None])

    def c_load(i, dep):
        return sch.op("Q", lambda q, i=i: q.dma_start(
            out=cbt[i % 3][:], in_=comb.rearrange("(n p) d -> n p d", p=128)[i]),
            deps=all_scats + [dep], dma=True)

    def r_load(i, dep):
        return sch.op("Q", lambda q, i=i: q.dma_start(
            out=rs_t[i % 3][:], in_=src_t[i]),
            deps=[all_scats[-1], dep], dma=True)

    for i in range(3):
        t_cl[i] = c_load(i, None)
        t_rl[i] = r_load(i, None)
    for i in range(NT):
        jb = i % 2
        t_ad[i] = sch.op("V", lambda v, i=i, jb=jb: v.tensor_tensor(
            out=onew[jb][:], in0=rs_t[i % 3][:], in1=cbt[i % 3][:], op=ALU.add),
            deps=[t_cl[i], t_rl[i], prevF["st"][jb]])
        if i + 3 < NT:
            t_cl[i + 3] = c_load(i + 3, t_ad[i])
            t_rl[i + 3] = r_load(i + 3, t_ad[i])
        t_st = sch.op("Q", lambda q, i=i, jb=jb: q.dma_start(
            out=dst_t[i], in_=onew[jb][:]), deps=[t_ad[i]], dma=True)
        prevF["st"][jb] = t_st
        t_out.append(t_st)
    return t_out





def emit_l1_attn(P, sch, src, dst, tag):
    c = P.consts
    CT = P.consts_tok
    TC = 256
    NCH = T // TC
    NDT = DIN // 128
    src_t = src.rearrange("(n p) d -> n p d", p=128)
    dst_t = dst.rearrange("(n p) d -> n p d", p=128)
    winT = P.dram["l1a_winT"]; woutT = P.dram["l1a_woutT"]
    wxT = P.dram["l1a_wxT"]; wdtT = P.dram["l1a_wdtT"]

    win_bf = P.sb(f"{tag}_winbf", [128, 8 * 4096], BF16)
    wout_bf = P.sb(f"{tag}_woutbf", [128, 16 * 1024], BF16)
    wx_bf = P.sb(f"{tag}_wxbf", [128, 16 * 96], BF16)
    wdt_bf = P.sb(f"{tag}_wdtbf", [64, 2048], BF16)
    convw = P.sb(f"{tag}_convw", [128, 64], F32)
    convb = P.sb(f"{tag}_convb", [128, 16], F32)
    bdt = P.sb(f"{tag}_bdt", [128, 16], F32)
    dp = P.sb(f"{tag}_dp", [128, 16], F32)
    A_sb = P.sb(f"{tag}_A", [128, 256], F32)
    sel16 = P.sb(f"{tag}_sel16", [16, 16 * 128], BF16)

    t_wl1 = sch.op("G", lambda g: g.dma_start(
        out=win_bf[:].rearrange("p (k m) -> p k m", k=8),
        in_=winT.rearrange("(k p) m -> p k m", p=128)), dma=True)
    t_wl4 = sch.op("G", lambda g: g.dma_start(
        out=wout_bf[:].rearrange("p (k m) -> p k m", k=16),
        in_=woutT.rearrange("(k p) m -> p k m", p=128)), deps=[t_wl1], dma=True)
    t_wl2 = sch.op("G", lambda g: g.dma_start(
        out=wx_bf[:].rearrange("p (k m) -> p k m", k=16),
        in_=wxT.rearrange("(k p) m -> p k m", p=128)), deps=[t_wl4], dma=True)
    t_wl3 = sch.op("G", lambda g: g.dma_start(
        out=wdt_bf[:], in_=wdtT), deps=[t_wl2], dma=True)
    t_sel = sch.op("G", lambda g: g.dma_start(
        out=sel16[:], in_=P.dram["l1a_sel16"]), deps=[t_wl3], dma=True)
    t_pl = sch.op("Q", lambda q: [
        q.dma_start(out=convw[:], in_=P.dram["l1a_convw"]),
        q.dma_start(out=convb[:], in_=P.dram["l1a_convb"]),
        q.dma_start(out=bdt[:], in_=P.dram["l1a_bdt"]),
        q.dma_start(out=dp[:], in_=P.dram["l1a_dp"]),
        q.dma_start(out=A_sb[:], in_=P.dram["l1a_alog"])],
        dma=True, ndma=5)
    t_A1 = sch.op("S", lambda s: s.activation(
        out=A_sb[:], in_=A_sb[:], func=AF.Exp), deps=[t_pl])
    t_A = sch.op("V", lambda v: v.tensor_scalar(
        out=A_sb[:], in0=A_sb[:], scalar1=-1.0, scalar2=None, op0=ALU.mult),
        deps=[t_A1], flush=True)

    x_t = [P.sb(f"{tag}_x{j}", [128, D], F32) for j in range(4)]
    xn_bf = P.sb(f"{tag}_xnbf", [128, D], BF16)
    w = ln_scratch(P, tag)
    w["prev_ln"] = CT
    xnc = P.sb(f"{tag}_xnc", [128, 8 * TC], BF16)
    xce = [P.sb(f"{tag}_xce{j}", [128, 3 + TC], F32) for j in range(2)]
    acc1 = P.sb(f"{tag}_acc1", [128, TC], F32)
    acc2 = P.sb(f"{tag}_acc2", [128, TC], F32)
    xconv = P.sb(f"{tag}_xconv", [128, NDT * TC], BF16)
    halo = P.sb(f"{tag}_halo", [128, NDT * 3], F32)
    dtlow_bf = P.sb(f"{tag}_dtlow", [64, TC], BF16)
    B_sb = P.sb(f"{tag}_Bsb", [16, TC], BF16)
    C_sb = P.sb(f"{tag}_Csb", [16, TC], BF16)
    B_bc = P.sb(f"{tag}_Bbc", [128, DSTATE * TC], BF16)
    C_bc = P.sb(f"{tag}_Cbc", [128, DSTATE * TC], BF16)
    dt_f = [P.sb(f"{tag}_dtf{j}", [128, TC], F32) for j in range(2)]
    u_bf = [P.sb(f"{tag}_u{j}", [128, TC], BF16) for j in range(2)]
    dA_f = [P.sb(f"{tag}_dA{j}", [128, TC], F32) for j in range(3)]
    dBx = [P.sb(f"{tag}_dBx{j}", [128, TC], BF16) for j in range(2)]
    hp = P.sb(f"{tag}_hp", [128, DSTATE * TC], BF16)
    state = P.sb(f"{tag}_state", [128, 256], F32)
    sz = P.sb(f"{tag}_sz", [128, TC], BF16)
    t1b = P.sb(f"{tag}_t1b", [128, TC], BF16)
    gated = P.sb(f"{tag}_gated", [128, NDT * TC], BF16)
    onew = P.sb(f"{tag}_onew", [128, D], F32)

    win3 = win_bf[:].rearrange("p (k m) -> p k m", k=8)
    wx3 = wx_bf[:].rearrange("p (k m) -> p k m", k=16)
    wout3 = wout_bf[:].rearrange("p (k m) -> p k m", k=16)
    xnc3 = xnc[:].rearrange("p (k t) -> p k t", k=8)

    p_tp = P.PB0[:, 0:D]
    p_xz = [P.PB1[:, 0:TC], P.PB1[:, 512:512 + TC]]
    p_dtl = P.PB2[0:64, 512:512 + TC]
    p_B = P.PB3[0:16, 512:512 + TC]
    p_C = P.PB0[0:16, 512:512 + TC]
    p_bc = [P.PB2[:, 0:TC], P.PB2[:, 256:256 + TC]]
    p_dt = [P.PB3[:, 0:TC], P.PB3[:, 256:256 + TC]]
    p_z = [P.PB2[:, 0:TC], P.PB2[:, 256:256 + TC]]
    p_out = [P.PB0[:, 0:1024], P.PB1[:, 0:1024]]

    t_out = []
    pv = dict(tp=None, xncev=None, xzev=[None, None], conv=None,
              xdblev=None, bcev=None, bcmm=None, dtev=[None, None],
              u=[None, None], dAr=[None, None, None], scan=None, cmul=None,
              zev=None, gate=None, add=[None, None], st=[None, None])
    xload = {}
    xread = {}

    def load_x(i, slot):
        t = sch.op("Q", lambda q, i=i, slot=slot: q.dma_start(
            out=x_t[slot][:], in_=src_t[i]), deps=[xread.get(slot)], dma=True)
        xload[i] = t

    for b in range(2):
        t_z0 = sch.op("V", lambda v: [
            v.memset(state[:], 0.0),
            v.memset(halo[:], 0.0),
            v.memset(xce[0][:, 0:3], 0.0)][-1], deps=[CT])
        for ci in range(NCH):
            cglob = b * NCH + ci
            for i in [b * 16 + ci * 2, b * 16 + ci * 2 + 1]:
                if i not in xload:
                    load_x(i, i % 4)
            nxt = cglob + 1
            if nxt < 16:
                nb, nci = nxt // NCH, nxt % NCH
                for i in [nb * 16 + nci * 2, nb * 16 + nci * 2 + 1]:
                    load_x(i, i % 4)
            for tloc in range(2):
                i = b * 16 + ci * 2 + tloc
                t_xn = emit_ln(sch, w, x_t[i % 4], xload[i], None,
                               xn_guard=pv["tp"], out_bf=xn_bf)
                t_tp = sch.op("T", lambda t: [
                    t.matmul(p_tp[:, k * 128:(k + 1) * 128],
                             xn_bf[:, k * 128:(k + 1) * 128], c["ident_bf"][:],
                             start=True, stop=True) for k in range(8)][-1],
                    deps=[t_xn, pv["xncev"], CT, pv["add"][0]])
                pv["tp"] = t_tp
                t_ev = sch.op("S", lambda s, tloc=tloc: s.activation(
                    out=xnc3[:, :, tloc * 128:(tloc + 1) * 128],
                    in_=p_tp[:].rearrange("p (k t) -> p k t", k=8),
                    func=AF.Copy), deps=[t_tp])
                pv["xncev"] = t_ev
            t_xnc = pv["xncev"]

            t_xdbl_last = None
            for m in range(NDT):
                jc = m % 2
                t_xzm = sch.op("T", lambda t, m=m, jc=jc: [
                    t.matmul(p_xz[jc], win3[:, k, m * 128:(m + 1) * 128],
                             xnc3[:, k, :], start=(k == 0), stop=(k == 7))
                    for k in range(8)][-1],
                    deps=[t_xnc, t_wl1, pv["xzev"][jc], pv["add"][1]])
                t_xce = sch.op("S", lambda s, jc=jc: s.activation(
                    out=xce[jc][:, 3:3 + TC], in_=p_xz[jc], func=AF.Copy),
                    deps=[t_xzm, pv["conv"]])
                pv["xzev"][jc] = t_xce

                def f_conv(v, m=m, jc=jc):
                    v.tensor_scalar(out=acc1[:], in0=xce[jc][:, 3:3 + TC],
                                    scalar1=convw[:, m * 4 + 3:m * 4 + 4],
                                    scalar2=None, op0=ALU.mult)
                    v.scalar_tensor_tensor(
                        out=acc2[:], in0=xce[jc][:, 2:2 + TC],
                        scalar=convw[:, m * 4 + 2:m * 4 + 3], in1=acc1[:],
                        op0=ALU.mult, op1=ALU.add)
                    v.scalar_tensor_tensor(
                        out=acc1[:], in0=xce[jc][:, 1:1 + TC],
                        scalar=convw[:, m * 4 + 1:m * 4 + 2], in1=acc2[:],
                        op0=ALU.mult, op1=ALU.add)
                    v.scalar_tensor_tensor(
                        out=acc2[:], in0=xce[jc][:, 0:TC],
                        scalar=convw[:, m * 4:m * 4 + 1], in1=acc1[:],
                        op0=ALU.mult, op1=ALU.add)
                    v.tensor_copy(out=halo[:, m * 3:(m + 1) * 3],
                                  in_=xce[jc][:, TC:TC + 3])
                    mn = (m + 1) % NDT
                    return v.tensor_copy(out=xce[(m + 1) % 2][:, 0:3],
                                         in_=halo[:, mn * 3:(mn + 1) * 3])
                t_cv = sch.op("V", f_conv, deps=[t_xce, t_pl, t_z0])
                t_xc = sch.op("S", lambda s, m=m: s.activation(
                    out=xconv[:, m * TC:(m + 1) * TC], in_=acc2[:],
                    func=AF.Silu, bias=convb[:, m:m + 1]),
                    deps=[t_cv, pv["xdblev"]])
                pv["conv"] = t_xc
                t_xd = sch.op("T", lambda t, m=m: [
                    t.matmul(p_dtl, wx3[:, m, 0:64],
                             xconv[:, m * TC:(m + 1) * TC],
                             start=(m == 0), stop=(m == NDT - 1)),
                    t.matmul(p_B, wx3[:, m, 64:80],
                             xconv[:, m * TC:(m + 1) * TC],
                             start=(m == 0), stop=(m == NDT - 1)),
                    t.matmul(p_C, wx3[:, m, 80:96],
                             xconv[:, m * TC:(m + 1) * TC],
                             start=(m == 0), stop=(m == NDT - 1))][-1],
                    deps=[t_xc, t_wl2, pv["xdblev"]])
                t_xdbl_last = t_xd

            def f_xde(s):
                s.activation(out=dtlow_bf[:], in_=p_dtl, func=AF.Copy)
                s.activation(out=B_sb[:], in_=p_B, func=AF.Copy)
                return s.activation(out=C_sb[:], in_=p_C, func=AF.Copy)
            t_xde = sch.op("S", f_xde, deps=[t_xdbl_last, pv["bcmm"],
                                             pv["dtev"][0], pv["dtev"][1],
                                             pv["zev"]])
            pv["xdblev"] = t_xde
            t_bcmm = None
            for n in range(2 * DSTATE):
                nn = n % DSTATE
                dstb = B_bc if n < DSTATE else C_bc
                rhs = B_sb if n < DSTATE else C_sb
                t_bm = sch.op("T", lambda t, n=n, nn=nn, rhs=rhs: t.matmul(
                    p_bc[n % 2], sel16[:, nn * 128:(nn + 1) * 128],
                    rhs[:], start=True, stop=True),
                    deps=[t_xde, t_sel, pv["bcev"]])
                t_be = sch.op("S", lambda s, n=n, nn=nn, dstb=dstb:
                    s.activation(out=dstb[:, nn * TC:(nn + 1) * TC],
                                 in_=p_bc[n % 2], func=AF.Copy),
                    deps=[t_bm, pv["cmul"]])
                pv["bcev"] = t_be
                t_bcmm = t_be
            pv["bcmm"] = t_bcmm

            gts = {}
            for m in range(NDT):
                jm = m % 2
                t_dtm = sch.op("T", lambda t, m=m, jm=jm: t.matmul(
                    p_dt[jm], wdt_bf[:, m * 128:(m + 1) * 128], dtlow_bf[:],
                    start=True, stop=True),
                    deps=[t_xde, t_wl3, pv["zev"]])

                def f_sp(s, m=m, jm=jm):
                    s.activation(out=dt_f[jm][:], in_=p_dt[jm], func=AF.Exp,
                                 bias=bdt[:, m:m + 1])
                    return s.activation(out=dt_f[jm][:], in_=dt_f[jm][:],
                                        func=AF.Ln, bias=c["one_col"][:, 0:1])
                t_sp = sch.op("S", f_sp, deps=[t_dtm, pv["u"][jm]])
                pv["dtev"][jm] = t_sp
                t_u = sch.op("V", lambda v, m=m, jm=jm: v.tensor_tensor(
                    out=u_bf[jm][:], in0=dt_f[jm][:],
                    in1=xconv[:, m * TC:(m + 1) * TC], op=ALU.mult),
                    deps=[t_sp])
                pv["u"][jm] = t_u
                t_scl = None
                for n in range(DSTATE):
                    jn3 = n % 3
                    jn2 = n % 2
                    t_dA = sch.op("S", lambda s, m=m, n=n, jm=jm, jn3=jn3:
                        s.activation(out=dA_f[jn3][:], in_=dt_f[jm][:],
                                     func=AF.Exp,
                                     scale=A_sb[:, m * 16 + n:m * 16 + n + 1]),
                        deps=[t_sp, t_A, pv["dAr"][jn3]])
                    t_dB = sch.op("V", lambda v, n=n, jm=jm, jn2=jn2:
                        v.tensor_tensor(out=dBx[jn2][:], in0=u_bf[jm][:],
                                        in1=B_bc[:, n * TC:(n + 1) * TC],
                                        op=ALU.mult), deps=[t_u, t_bcmm])
                    t_sc = sch.op("V", lambda v, m=m, n=n, jn3=jn3, jn2=jn2:
                        v.tensor_tensor_scan(
                            out=hp[:, n * TC:(n + 1) * TC],
                            data0=dA_f[jn3][:], data1=dBx[jn2][:],
                            initial=state[:, m * 16 + n:m * 16 + n + 1],
                            op0=ALU.mult, op1=ALU.add),
                        deps=[t_dA, t_dB, t_z0])
                    pv["dAr"][jn3] = t_sc
                    pv["scan"] = t_sc
                    t_scl = t_sc
                t_sv = sch.op("V", lambda v, m=m: v.tensor_copy(
                    out=state[:, m * 16:(m + 1) * 16].rearrange(
                        "p (n o) -> p n o", o=1),
                    in_=hp[:].rearrange("p (n t) -> p n t", n=DSTATE)
                        [:, :, TC - 1:TC]), deps=[t_scl])

                def f_y(v, m=m):
                    v.tensor_tensor(out=hp[:], in0=hp[:], in1=C_bc[:],
                                    op=ALU.mult)
                    half = DSTATE * TC // 2
                    r = None
                    while half >= TC:
                        r = v.tensor_tensor(out=hp[:, 0:half], in0=hp[:, 0:half],
                                            in1=hp[:, half:2 * half],
                                            op=ALU.add)
                        half //= 2
                    return r
                t_y = sch.op("V", f_y, deps=[t_scl, t_bcmm, t_sv])
                pv["cmul"] = t_y
                t_zm = sch.op("T", lambda t, m=m, jm=jm: [
                    t.matmul(p_z[jm], win3[:, k, (16 + m) * 128:(17 + m) * 128],
                             xnc3[:, k, :], start=(k == 0), stop=(k == 7))
                    for k in range(8)][-1],
                    deps=[t_xnc, t_wl1, t_sp, pv["zev"], t_bcmm])
                t_sz = sch.op("S", lambda s, jm=jm: s.activation(
                    out=sz[:], in_=p_z[jm], func=AF.Silu),
                    deps=[t_zm, pv["gate"]])
                pv["zev"] = t_sz

                def f_gate(v, m=m):
                    v.scalar_tensor_tensor(
                        out=t1b[:], in0=xconv[:, m * TC:(m + 1) * TC],
                        scalar=dp[:, m:m + 1], in1=hp[:, 0:TC],
                        op0=ALU.mult, op1=ALU.add)
                    return v.tensor_tensor(
                        out=gated[:, m * TC:(m + 1) * TC], in0=t1b[:],
                        in1=sz[:], op=ALU.mult)
                t_g = sch.op("V", f_gate, deps=[t_sz, t_y, t_pl])
                gts[m] = t_g
                pv["gate"] = t_g

            mml = [None, None]
            for k in range(NDT):
                for mt in range(2):
                    mml[mt] = sch.op("T", lambda t, k=k, mt=mt: [
                        t.matmul(
                            p_out[mt][:, nh * 512:(nh + 1) * 512],
                            gated[:, k * TC + mt * 128:k * TC + (mt + 1) * 128],
                            wout3[:, k, nh * 512:(nh + 1) * 512],
                            start=(k == 0), stop=(k == NDT - 1))
                        for nh in range(2)][-1],
                        deps=[gts[k], t_wl4,
                              pv["add"][mt] if k == 0 else None])
            for mt in range(2):
                i = b * 16 + ci * 2 + mt
                t_ad = sch.op("V", lambda v, mt=mt, i=i: v.tensor_tensor(
                    out=onew[:], in0=x_t[i % 4][:], in1=p_out[mt],
                    op=ALU.add),
                    deps=[mml[mt], pv["st"][1 - mt], pv["st"][mt]])
                pv["add"][mt] = t_ad
                xread[i % 4] = t_ad
                t_st = sch.op("Q", lambda q, i=i: q.dma_start(
                    out=dst_t[i], in_=onew[:]), deps=[t_ad], dma=True)
                pv["st"][mt] = t_st
                t_out.append(t_st)
    return t_out



def _prep_weights(inp):
    w = {}
    f32c = lambda a: np.ascontiguousarray(np.asarray(a), dtype=np.float32)
    w["l0_gwT"] = f32c(np.asarray(inp["L0_ffn_gate_w"]).T)
    w["l0_wgT"] = f32c(np.asarray(inp["L0_ffn_w_gate"]).transpose(0, 2, 1))
    w["l0_wuT"] = f32c(np.asarray(inp["L0_ffn_w_up"]).transpose(0, 2, 1))
    w["l0_wdT"] = f32c(np.asarray(inp["L0_ffn_w_down"]).transpose(0, 2, 1))
    w["l0a_wq"] = f32c(inp["L0_attn_Wq"])
    w["l0a_wk"] = f32c(inp["L0_attn_Wk"])
    w["l0a_wv"] = f32c(inp["L0_attn_Wv"])
    w["l0a_wp"] = f32c(inp["L0_attn_Wp"])
    w["l0a_bq"] = f32c(np.asarray(inp["L0_attn_bq"]).reshape(8, 128).T)
    w["l0a_bk"] = f32c(np.asarray(inp["L0_attn_bk"]).reshape(8, 128).T)
    w["l0a_bv"] = f32c(np.asarray(inp["L0_attn_bv"]).reshape(8, 128).T)
    w["l0a_bp"] = f32c(np.asarray(inp["L0_attn_bp"]).reshape(1, 1024))
    w["l1a_winT"] = f32c(np.asarray(inp["L1_attn_W_in"]).T)
    w["l1a_woutT"] = f32c(np.asarray(inp["L1_attn_W_out"]).T)
    w["l1a_wxT"] = f32c(np.asarray(inp["L1_attn_W_x"]).T)
    w["l1a_wdtT"] = f32c(np.asarray(inp["L1_attn_W_dt"]).T)
    w["l1a_convw"] = f32c(np.asarray(inp["L1_attn_conv_w"]).reshape(
        16, 128, 4).transpose(1, 0, 2).reshape(128, 64))
    w["l1a_convb"] = f32c(np.asarray(inp["L1_attn_conv_b"]).reshape(16, 128).T)
    w["l1a_bdt"] = f32c(np.asarray(inp["L1_attn_b_dt"]).reshape(16, 128).T)
    w["l1a_dp"] = f32c(np.asarray(inp["L1_attn_Dp"]).reshape(16, 128).T)
    w["l1a_alog"] = f32c(np.asarray(inp["L1_attn_A_log"]).reshape(
        16, 128, 16).transpose(1, 0, 2).reshape(128, 256))
    sel = np.zeros((16, 16 * 128), np.float32)
    for n in range(16):
        sel[n, n * 128:(n + 1) * 128] = 1.0
    w["l1a_sel16"] = sel
    w["l1_rw1"] = f32c(inp["L1_ffn_router_w1"])
    w["l1_rw2"] = f32c(inp["L1_ffn_router_w2"])
    w["l1_w1"] = f32c(inp["L1_ffn_w1"])
    w["l1_w2"] = f32c(inp["L1_ffn_w2"])
    w["l1_w3"] = f32c(inp["L1_ffn_w3"])
    return w


def _declare_common(P, wnames):
    shapes = {
        "l0_gwT": ([1024, 8], F32),
        "l0_wgT": ([E0, 1024, I0], F32),
        "l0_wuT": ([E0, 1024, I0], F32),
        "l0_wdT": ([E0, I0, 1024], F32),
        "l0a_wq": ([1024, 1024], F32),
        "l0a_wk": ([1024, 1024], F32),
        "l0a_wv": ([1024, 1024], F32),
        "l0a_wp": ([1024, 1024], F32),
        "l0a_bq": ([128, 8], F32),
        "l0a_bk": ([128, 8], F32),
        "l0a_bv": ([128, 8], F32),
        "l0a_bp": ([1, 1024], F32),
        "l1a_winT": ([1024, 2 * DIN], F32),
        "l1a_woutT": ([DIN, 1024], F32),
        "l1a_wxT": ([DIN, DTRANK + 2 * DSTATE], F32),
        "l1a_wdtT": ([DTRANK, DIN], F32),
        "l1a_convw": ([128, 64], F32),
        "l1a_convb": ([128, 16], F32),
        "l1a_bdt": ([128, 16], F32),
        "l1a_dp": ([128, 16], F32),
        "l1a_alog": ([128, 256], F32),
        "l1a_sel16": ([16, 2048], F32),
        "l1_rw1": ([1024, 128], F32),
        "l1_rw2": ([128, 16], F32),
        "l1_w1": ([E1, 1024, DFF1], F32),
        "l1_w2": ([E1, 1024, DFF1], F32),
        "l1_w3": ([E1, DFF1, 1024], F32),
    }
    for n in wnames:
        sh, dt = shapes[n]
        P.dram_in(n, sh, dt)


def build_l0ffn_only(stages="ABC", dbg=False):
    P = Prog()
    P.dram_in("rsrc", [TOK, D], F32)
    P.dram_out("rdst", [TOK, D], F32)
    if dbg:
        P.dram_out("dbg_gidx", [128, NT], F32)
        P.dram_out("dbg_esel", [128, NT], F32)
        P.dram_out("dbg_rsel", [128, NT], F32)
        P.dram_out("dbg_lgmx", [128, NT], F32)
        P.dram_out("dbg_lg", [128, NT * E0], F32)
        P.dram_out("dbg_fm", [128, D], F32)
        P.dram_out("dbg_gw", [128, 8 * E0], F32)
        P.dram_out("dbg_slots", [SLOTS0, D], BF16)
        P.dram_out("dbg_oslots", [SLOTS0 + 1, D], F32)
    _declare_common(P, ["l0_gwT", "l0_wgT", "l0_wuT", "l0_wdT"])
    P.dram_tmp("l0_slots", [SLOTS0, D], BF16)
    P.dram_tmp("l0_oslots", [SLOTS0 + 1, D], F32)
    P.open_block()
    P.make_psum_pool()
    sch = Sched(P)
    make_consts(P, sch)
    t_out = emit_l0_ffn(P, sch, P.dram["rsrc"], P.dram["rdst"], "f0",
                        stages=stages)
    if dbg:
        bb = P.sb("dbgb", [128, D], F32)
        prev = None
        for zi in range(SLOTS0 // 128):
            t1 = sch.op("Q", lambda q, zi=zi: q.dma_start(
                out=bb[:].bitcast(BF16)[:, 0:D] if False else bb[:, 0:D // 2].bitcast(BF16) if False else bb[:],
                in_=P.dram["l0_oslots"][zi * 128:(zi + 1) * 128, :]),
                deps=[prev] + t_out, dma=True)
            t2 = sch.op("Q", lambda q, zi=zi: q.dma_start(
                out=P.dram["dbg_oslots"][zi * 128:(zi + 1) * 128, :], in_=bb[:]),
                deps=[t1], dma=True)
            prev = t2
            t_out = t_out + [t2]
        bb2 = P.sb("dbgb2", [128, D], BF16)
        for zi in range(SLOTS0 // 128):
            t1 = sch.op("Q", lambda q, zi=zi: q.dma_start(
                out=bb2[:], in_=P.dram["l0_slots"][zi * 128:(zi + 1) * 128, :]),
                deps=[prev], dma=True)
            t2 = sch.op("Q", lambda q, zi=zi: q.dma_start(
                out=P.dram["dbg_slots"][zi * 128:(zi + 1) * 128, :], in_=bb2[:]),
                deps=[t1], dma=True)
            prev = t2
            t_out = t_out + [t2]
        t3 = sch.op("Q", lambda q: q.dma_start(
            out=P.dram["dbg_gidx"], in_=P._dbg_gidx[:]), deps=[prev], dma=True)
        t4 = sch.op("Q", lambda q: q.dma_start(
            out=P.dram["dbg_esel"], in_=P._dbg_esel[:]), deps=[t3], dma=True)
        t5 = sch.op("Q", lambda q: q.dma_start(
            out=P.dram["dbg_rsel"], in_=P._dbg_rsel[:]), deps=[t4], dma=True)
        t6 = sch.op("Q", lambda q: q.dma_start(
            out=P.dram["dbg_lgmx"], in_=P._dbg_lgmx[:]), deps=[t5], dma=True)
        t7 = sch.op("Q", lambda q: q.dma_start(
            out=P.dram["dbg_lg"], in_=P._dbg_lg[:]), deps=[t6], dma=True)
        t8 = sch.op("Q", lambda q: q.dma_start(
            out=P.dram["dbg_fm"], in_=P._dbg_fm[:]), deps=[t7], dma=True)
        t9 = sch.op("Q", lambda q: q.dma_start(
            out=P.dram["dbg_gw"], in_=P._dbg_gw[:]), deps=[t8], dma=True)
        t_out = t_out + [t3, t4, t5, t6, t7, t8, t9]
    sch.op("G", lambda g: g.memset(P.consts["big8"][:], 999.0), deps=t_out)
    sch.emit()
    P.close()
    return P


def run_l0ffn(rsrc_full, inp, stages="ABC", dbg=False, ncores=NCORES):
    w = _prep_weights(inp)
    P = build_l0ffn_only(stages, dbg=dbg)
    in_maps = []
    for core in range(ncores):
        m = {"rsrc": np.ascontiguousarray(
            rsrc_full[core * BPC:(core + 1) * BPC].reshape(TOK, D))}
        for k in ("l0_gwT", "l0_wgT", "l0_wuT", "l0_wdT"):
            m[k] = w[k]
        in_maps.append(m)
    res = run_bass_kernel_spmd(P.nc, in_maps, core_ids=list(range(ncores)))
    if dbg:
        return res
    out = np.stack([res.results[c]["rdst"].reshape(BPC, T, D)
                    for c in range(ncores)])
    return out.reshape(BATCH if ncores == NCORES else ncores * BPC, T, D)


L0A_WN = ["l0a_wq", "l0a_wk", "l0a_wv", "l0a_wp", "l0a_bq", "l0a_bk",
          "l0a_bv", "l0a_bp"]


def build_l0attn_only(dbg=False):
    P = Prog()
    P.dram_in("rsrc", [TOK, D], F32)
    P.dram_out("rdst", [TOK, D], F32)
    _declare_common(P, L0A_WN)
    if dbg:
        P.dram_out("a0_attn", [128, 8, TOK], BF16)
    else:
        P.dram_tmp("a0_attn", [128, 8, TOK], BF16)
    P.open_block()
    P.make_psum_pool()
    sch = Sched(P)
    make_consts(P, sch)
    t_out = emit_l0_attn(P, sch, P.dram["rsrc"], P.dram["rdst"], "a0")
    sch.op("G", lambda g: g.memset(P.consts["big8"][:], 999.0), deps=t_out)
    sch.emit()
    P.close()
    return P


def run_l0attn(rsrc_full, inp, ncores=NCORES, dbg=False):
    w = _prep_weights(inp)
    P = build_l0attn_only(dbg)
    in_maps = []
    for core in range(ncores):
        m = {"rsrc": np.ascontiguousarray(
            rsrc_full[core * BPC:(core + 1) * BPC].reshape(TOK, D))}
        for k in L0A_WN:
            m[k] = w[k]
        in_maps.append(m)
    res = run_bass_kernel_spmd(P.nc, in_maps, core_ids=list(range(ncores)))
    if dbg:
        return res
    out = np.stack([res.results[c]["rdst"].reshape(BPC, T, D)
                    for c in range(ncores)])
    return out.reshape(ncores * BPC, T, D)


L1F_WN = ["l1_rw1", "l1_rw2", "l1_w1", "l1_w2", "l1_w3"]


def build_l1ffn_only(dbg=False):
    P = Prog()
    P.dram_in("rsrc", [TOK, D], F32)
    P.dram_out("rdst", [TOK, D], F32)
    _declare_common(P, L1F_WN)
    P.dram_tmp("f1_xnorm", [TOK, D], BF16)
    if dbg:
        P.dram_out("f1_comb", [TOK, D], F32)
        P.dram_out("dbg_idx", [128, 64], I32)
        P.dram_out("dbg_wcol", [128, 64], F32)
        P.dram_out("dbg_LG", [64, T], F32)
        P.dram_out("dbg_ranktm", [128, NTB * 64], F32)
    else:
        P.dram_tmp("f1_comb", [TOK, D], F32)
    P.open_block()
    P.make_psum_pool()
    sch = Sched(P)
    make_consts(P, sch)
    t_out = emit_l1_ffn(P, sch, P.dram["rsrc"], P.dram["rdst"], "f1")
    if dbg:
        d = P._l1dbg
        prev = t_out[-1]
        for nm, buf in (("dbg_idx", d["idx"]), ("dbg_wcol", d["wcol"]),
                        ("dbg_LG", d["LG"]), ("dbg_ranktm", d["rank_tm"])):
            prev = sch.op("Q", lambda q, nm=nm, buf=buf: q.dma_start(
                out=P.dram[nm], in_=buf[:]), deps=[prev] + t_out, dma=True)
        t_out = t_out + [prev]
    sch.op("G", lambda g: g.memset(P.consts["big8"][:], 999.0), deps=t_out)
    sch.emit()
    P.close()
    return P


def run_l1ffn(rsrc_full, inp, ncores=NCORES, dbg=False):
    w = _prep_weights(inp)
    P = build_l1ffn_only(dbg)
    in_maps = []
    for core in range(ncores):
        m = {"rsrc": np.ascontiguousarray(
            rsrc_full[core * BPC:(core + 1) * BPC].reshape(TOK, D))}
        for k in L1F_WN:
            m[k] = w[k]
        in_maps.append(m)
    res = run_bass_kernel_spmd(P.nc, in_maps, core_ids=list(range(ncores)))
    if dbg:
        return res
    out = np.stack([res.results[c]["rdst"].reshape(BPC, T, D)
                    for c in range(ncores)])
    return out.reshape(ncores * BPC, T, D)


L1A_WN = ["l1a_winT", "l1a_woutT", "l1a_wxT", "l1a_wdtT", "l1a_convw",
          "l1a_convb", "l1a_bdt", "l1a_dp", "l1a_alog", "l1a_sel16"]


def build_l1attn_only():
    P = Prog()
    P.dram_in("rsrc", [TOK, D], F32)
    P.dram_out("rdst", [TOK, D], F32)
    _declare_common(P, L1A_WN)
    P.open_block()
    P.make_psum_pool()
    sch = Sched(P)
    make_consts(P, sch, lite=True)
    t_out = emit_l1_attn(P, sch, P.dram["rsrc"], P.dram["rdst"], "a1")
    sch.op("G", lambda g: g.memset(P.consts["big8"][:], 999.0), deps=t_out)
    sch.emit()
    P.close()
    return P


def run_l1attn(rsrc_full, inp, ncores=NCORES):
    w = _prep_weights(inp)
    P = build_l1attn_only()
    in_maps = []
    for core in range(ncores):
        m = {"rsrc": np.ascontiguousarray(
            rsrc_full[core * BPC:(core + 1) * BPC].reshape(TOK, D))}
        for k in L1A_WN:
            m[k] = w[k]
        in_maps.append(m)
    res = run_bass_kernel_spmd(P.nc, in_maps, core_ids=list(range(ncores)))
    out = np.stack([res.results[c]["rdst"].reshape(BPC, T, D)
                    for c in range(ncores)])
    return out.reshape(ncores * BPC, T, D)


L0F_WN = ["l0_gwT", "l0_wgT", "l0_wuT", "l0_wdT"]

_BUILD_CACHE = {}


def _get_prog(name, fn):
    if name not in _BUILD_CACHE:
        _BUILD_CACHE[name] = fn()
    return _BUILD_CACHE[name]


def _run_chain_step(P, rsrc_full, w, wn):
    in_maps = []
    for core in range(NCORES):
        m = {"rsrc": np.ascontiguousarray(
            rsrc_full[core * BPC:(core + 1) * BPC].reshape(TOK, D))}
        for k in wn:
            m[k] = w[k]
        in_maps.append(m)
    res = run_bass_kernel_spmd(P.nc, in_maps, core_ids=list(range(NCORES)))
    return np.stack([res.results[c]["rdst"].reshape(BPC, T, D)
                     for c in range(NCORES)]).reshape(NCORES * BPC, T, D)


def kernel(**inputs):
    w = _prep_weights(inputs)
    x = np.asarray(inputs["L0_attn_x"], np.float32)
    P0f = _get_prog("l0f", build_l0ffn_only)
    P0a = _get_prog("l0a", build_l0attn_only)
    P1f = _get_prog("l1f", build_l1ffn_only)
    P1a = _get_prog("l1a", build_l1attn_only)
    r = _run_chain_step(P0f, x, w, L0F_WN)
    r = _run_chain_step(P0a, r, w, L0A_WN)
    r = _run_chain_step(P0f, r, w, L0F_WN)
    r = _run_chain_step(P1f, r, w, L1F_WN)
    r = _run_chain_step(P1a, r, w, L1A_WN)
    r = _run_chain_step(P1f, r, w, L1F_WN)
    return np.ascontiguousarray(r.astype(np.float32))
